# version 03
# baseline (speedup 1.0000x reference)
import contextlib

import numpy as np
import ml_dtypes
import concourse.bass as bass
import concourse.mybir as mybir
from concourse.bass_utils import run_bass_kernel_spmd

F32 = mybir.dt.float32
BF16 = mybir.dt.bfloat16
AF = mybir.ActivationFunctionType
ALU = mybir.AluOpType
AX = mybir.AxisListType.X

T = 4096
D = 1024
DC = D // 128
TT = 512
NTILE = T // TT
N_CORES = 8

f32 = lambda a: np.ascontiguousarray(a, np.float32)


ENGS = ("tensor", "scalar", "vector", "gpsimd", "sync", "scalar_dma")


class Prog:

    def __init__(self, block, sems):
        self.block = block
        self.sems = sems
        self.ops = []
        self.tot = {e: 0 for e in ENGS}
        self.last_write = {}
        self.readers = {}
        self.barrier_ms = {e: 0 for e in ENGS}
        self.engines = set()

    def op(self, eng, fn, reads=None, writes=(), inc=1):
        deps = dict(self.barrier_ms)
        if reads is None:
            for e in ENGS:
                deps[e] = max(deps[e], self.tot[e])
        else:
            for k in reads:
                d = self.last_write.get(k)
                if d:
                    deps[d[0]] = max(deps[d[0]], d[1])
            for k in writes:
                d = self.last_write.get(k)
                if d:
                    deps[d[0]] = max(deps[d[0]], d[1])
                for d in self.readers.get(k, ()):
                    deps[d[0]] = max(deps[d[0]], d[1])
        ms = self.tot[eng] + inc
        self.tot[eng] = ms
        self.ops.append((eng, fn, inc, deps))
        if reads is None:
            for e in ENGS:
                self.barrier_ms[e] = self.tot[e]
            self.last_write.clear()
            self.readers.clear()
        else:
            me = (eng, ms)
            for k in reads:
                self.readers.setdefault(k, []).append(me)
            for k in writes:
                self.last_write[k] = me
                self.readers[k] = []

    def mm(self, fn, reads=None, writes=()):
        self.op("tensor", fn, reads, writes)

    def act(self, fn, reads=None, writes=()):
        self.op("scalar", fn, reads, writes)

    def dve(self, fn, reads=None, writes=()):
        self.op("vector", fn, reads, writes)

    def pool(self, fn, reads=None, writes=()):
        self.op("gpsimd", fn, reads, writes)

    def dma(self, fn, reads=None, writes=()):
        self.op("sync", fn, reads, writes, inc=16)

    def dma2(self, fn, reads=None, writes=()):
        self.op("scalar_dma", fn, reads, writes, inc=16)

    def flush(self):
        stream_of = lambda e: "scalar" if e == "scalar_dma" else e
        by_stream = {}
        for item in self.ops:
            by_stream.setdefault(stream_of(item[0]), []).append(item)
        self.ops = []
        for stream, items in by_stream.items():
            self.engines.add(stream)
            dec = getattr(self.block, stream)

            def body(engine, items=items):
                waited = {e: 0 for e in ENGS}
                for (_en, fn, inc, deps) in items:
                    for de, dv in deps.items():
                        if dv <= waited[de]:
                            continue
                        if de == _en == "tensor":
                            continue
                        engine.wait_ge(self.sems[de], dv)
                        waited[de] = dv
                    fn(engine).then_inc(self.sems[_en], inc)

            dec(body)

    def barrier(self, h):
        self.op("vector", lambda e: e.memset(h["s_mean"][:], 0.0), reads=None)

    def finish(self):
        self.flush()
        for stream in self.engines:
            dec = getattr(self.block, stream)

            def fin(engine):
                for e in ENGS:
                    if self.tot[e]:
                        engine.wait_ge(self.sems[e], self.tot[e])

            dec(fin)



def alloc_common(nc, st):
    h = {}
    h["ones_col"] = st.enter_context(nc.sbuf_tensor("s_ones_col", [128, 1], F32))
    h["ones_row"] = st.enter_context(nc.sbuf_tensor("s_ones_row", [1, 128], F32))
    h["idf"] = st.enter_context(nc.sbuf_tensor("s_idf", [128, 128], F32))
    h["cst"] = st.enter_context(nc.sbuf_tensor("s_cst", [128, 8], F32))
    h["sel"] = st.enter_context(nc.sbuf_tensor("s_sel", [16, 16 * 128], F32))
    return h


def common_inputs():
    cst = np.zeros((128, 8), np.float32)
    cst[:, 0] = 1.0
    cst[:, 1] = 1e-5
    sel = np.zeros((16, 16 * 128), np.float32)
    for e in range(16):
        sel[e, e * 128:(e + 1) * 128] = 1.0
    return {
        "c_sel": sel,
        "c_ones_col": f32(np.ones((128, 1))),
        "c_ones_row": f32(np.ones((1, 128))),
        "c_idf": f32(np.eye(128)),
        "c_cst": cst,
    }


def declare_common(nc, p, h):
    shp = {"c_ones_col": [128, 1], "c_ones_row": [1, 128],
           "c_idf": [128, 128], "c_cst": [128, 8], "c_sel": [16, 16 * 128]}
    key = {"c_ones_col": "ones_col", "c_ones_row": "ones_row",
           "c_idf": "idf", "c_cst": "cst", "c_sel": "sel"}
    for name in shp:
        ap = nc.dram_tensor(name, shp[name], F32, kind="ExternalInput").ap()
        sb = h[key[name]]
        p.dma(lambda e, sb=sb, ap=ap: e.dma_start(out=sb[:], in_=ap),
              reads=[], writes=["const"])


def alloc_ln(nc, st, h):
    h["p_stat1"] = st.enter_context(nc.psum_tensor("p_stat1", [1, TT], F32))
    h["p_stat2"] = st.enter_context(nc.psum_tensor("p_stat2", [1, TT], F32))
    h["p_bc"] = st.enter_context(nc.psum_tensor("p_bc", [128, TT], F32))
    for n, shp in [("s_mean", [1, TT]), ("s_msq", [1, TT]), ("s_var", [1, TT]),
                   ("s_sd", [1, TT]), ("s_rstd", [1, TT]), ("s_beta", [1, TT]),
                   ("s_sq", [128, TT]), ("s_abc", [128, TT]), ("s_bbc", [128, TT])]:
        h[n] = st.enter_context(nc.sbuf_tensor(n, shp, F32))


def alloc_gate(nc, st, h, pfx):
    for n, shp in [("s_Lg", [128, 8]), ("s_m1", [128, 1]), ("s_nm1", [128, 1]),
                   ("s_eq", [128, 8]), ("s_L2", [128, 8]), ("s_m2", [128, 1]),
                   ("s_em2", [128, 1]), ("s_den", [128, 1]), ("s_rd", [128, 1]),
                   ("s_geq", [128, 8]), ("s_c", [128, 8]), ("s_e", [128, 8]),
                   ("s_Z", [128, 1])]:
        h[n] = st.enter_context(nc.sbuf_tensor(pfx + n, shp, F32))


def emit_ln(p, h, sb_x, sb_xlf, sb_xlb=None, xk="x"):
    pm, ps, pbc = h["p_stat1"], h["p_stat2"], h["p_bc"]
    s_mean, s_msq, s_var, s_sd, s_rstd, s_beta = (
        h["s_mean"], h["s_msq"], h["s_var"], h["s_sd"], h["s_rstd"], h["s_beta"])
    ones_col, ones_row, cst = h["ones_col"], h["ones_row"], h["cst"]
    sq, abc, bbc = h["s_sq"], h["s_abc"], h["s_bbc"]
    for c in range(DC):
        p.mm(lambda e, c=c: e.matmul(pm[:], ones_col[:], sb_x[:, c, :],
                                     start=(c == 0), stop=(c == DC - 1)),
             reads=[xk, "const", "mean"], writes=["p_s1"])
    p.dve(lambda e: e.tensor_scalar(s_mean[:], pm[:], 1.0 / D, None, ALU.mult),
          reads=["p_s1"], writes=["mean"])
    for c in range(DC):
        p.act(lambda e, c=c: e.activation(sq[:], sb_x[:, c, :], AF.Square),
              reads=[xk], writes=["sq"])
        p.mm(lambda e, c=c: e.matmul(ps[:], ones_col[:], sq[:],
                                     start=(c == 0), stop=(c == DC - 1)),
             reads=["sq", "const"], writes=["p_s2"])
    p.act(lambda e: e.activation(s_msq[:], s_mean[:], AF.Square),
          reads=["mean"], writes=["msq"])
    p.dve(lambda e: e.scalar_tensor_tensor(s_var[:], ps[:], 1.0 / D, s_msq[:],
                                           ALU.mult, ALU.subtract),
          reads=["p_s2", "msq"], writes=["var"])
    p.act(lambda e: e.activation(s_sd[:], s_var[:], AF.Sqrt, bias=cst[:1, 1:2]),
          reads=["var", "const"], writes=["sd"])
    p.dve(lambda e: e.reciprocal(s_rstd[:], s_sd[:]), reads=["sd"], writes=["rstd"])
    p.dve(lambda e: e.scalar_tensor_tensor(s_beta[:], s_mean[:], -1.0, s_rstd[:],
                                           ALU.mult, ALU.mult),
          reads=["mean", "rstd"], writes=["beta"])
    p.mm(lambda e: e.matmul(pbc[:], ones_row[:], s_rstd[:], start=True, stop=True),
         reads=["rstd", "const"], writes=["p_bc"])
    p.act(lambda e: e.activation(abc[:], pbc[:], AF.Copy),
          reads=["p_bc"], writes=["abc"])
    p.mm(lambda e: e.matmul(pbc[:], ones_row[:], s_beta[:], start=True, stop=True),
         reads=["beta", "const"], writes=["p_bc"])
    p.act(lambda e: e.activation(bbc[:], pbc[:], AF.Copy),
          reads=["p_bc"], writes=["bbc"])
    for c in range(DC):
        if sb_xlf is None:
            p.dve(lambda e, c=c: e.tensor_tensor(sq[:], sb_x[:, c, :],
                                                 abc[:], ALU.mult),
                  reads=[xk, "abc"], writes=["sq"])
            p.dve(lambda e, c=c: e.tensor_tensor(sb_xlb[:, c, :], sq[:],
                                                 bbc[:], ALU.add),
                  reads=["sq", "bbc"], writes=["xlb"])
            continue
        p.dve(lambda e, c=c: e.tensor_tensor(sb_xlf[:, c, :], sb_x[:, c, :],
                                             abc[:], ALU.mult),
              reads=[xk, "abc"], writes=["xlf"])
        p.dve(lambda e, c=c: e.tensor_tensor(sb_xlf[:, c, :], sb_xlf[:, c, :],
                                             bbc[:], ALU.add),
              reads=["bbc", "xlf"], writes=["xlf"])
        if sb_xlb is not None:
            p.dve(lambda e, c=c: e.tensor_copy(sb_xlb[:, c, :], sb_xlf[:, c, :]),
                  reads=["xlf"], writes=["xlb"])


def emit_top2(p, h, sb_logT, sb_cmb, use_router_eps):
    idf = h["idf"]
    Lg, m1, nm1, eq, L2, m2, em2, den, rd, geq, ec, ee, Zs = (
        h["s_Lg"], h["s_m1"], h["s_nm1"], h["s_eq"], h["s_L2"], h["s_m2"],
        h["s_em2"], h["s_den"], h["s_rd"], h["s_geq"], h["s_c"], h["s_e"], h["s_Z"])
    ptm = h["p_bc"][:, 0:8]
    pcmb = h["p_bc"][0:8, 0:128]
    K = "p_bc"
    for gq in range(TT // 128):
        p.mm(lambda e, gq=gq: e.transpose(ptm, sb_logT[:, gq * 128:(gq + 1) * 128],
                                          idf[:8, :8]),
             reads=["logT", "const"], writes=[K])
        p.dve(lambda e: e.tensor_copy(Lg[:], ptm), reads=[K], writes=["gLg"])
        p.dve(lambda e: e.tensor_reduce(m1[:], Lg[:], AX, ALU.max),
              reads=["gLg"], writes=["gm1"])
        p.dve(lambda e: e.tensor_scalar(nm1[:], m1[:], -1.0, None, ALU.mult),
              reads=["gm1"], writes=["gnm1"])
        p.act(lambda e: e.activation(ee[:], Lg[:], AF.Exp, bias=nm1[:]),
              reads=["gLg", "gnm1"], writes=["gee"])
        p.dve(lambda e: e.tensor_scalar(eq[:], Lg[:], m1[:], None, ALU.is_equal),
              reads=["gLg", "gm1"], writes=["geq1"])
        p.dve(lambda e: e.scalar_tensor_tensor(L2[:], eq[:], -1e30, Lg[:],
                                               ALU.mult, ALU.add),
              reads=["geq1", "gLg"], writes=["gL2"])
        p.dve(lambda e: e.tensor_reduce(m2[:], L2[:], AX, ALU.max),
              reads=["gL2"], writes=["gm2"])
        p.act(lambda e: e.activation(em2[:], m2[:], AF.Exp, bias=nm1[:]),
              reads=["gm2", "gnm1"], writes=["gem2"])
        if use_router_eps:
            p.dve(lambda e: e.tensor_reduce(Zs[:], ee[:], AX, ALU.add),
                  reads=["gee"], writes=["gZ"])
            p.dve(lambda e: e.scalar_tensor_tensor(den[:], Zs[:], 1e-6, em2[:],
                                                   ALU.mult, ALU.add),
                  reads=["gZ", "gem2"], writes=["gden"])
            p.dve(lambda e: e.tensor_scalar(den[:], den[:], 1.0, None, ALU.add),
                  reads=["gden"], writes=["gden"])
        else:
            p.dve(lambda e: e.tensor_scalar(den[:], em2[:], 1.0, None, ALU.add),
                  reads=["gem2"], writes=["gden"])
        p.dve(lambda e: e.reciprocal(rd[:], den[:]), reads=["gden"], writes=["grd"])
        p.dve(lambda e: e.tensor_scalar(rd[:], rd[:], 0.5, None, ALU.mult),
              reads=["grd"], writes=["grd"])
        p.dve(lambda e: e.tensor_scalar(geq[:], Lg[:], m2[:], None, ALU.is_ge),
              reads=["gLg", "gm2"], writes=["ggeq"])
        p.dve(lambda e: e.tensor_tensor(ec[:], ee[:], geq[:], ALU.mult),
              reads=["gee", "ggeq"], writes=["gec"])
        p.dve(lambda e: e.tensor_scalar(ec[:], ec[:], rd[:], None, ALU.mult),
              reads=["grd", "gec"], writes=["gec"])
        p.mm(lambda e, gq=gq: e.transpose(pcmb, ec[:], idf[:]),
             reads=["gec", "const"], writes=[K])
        p.dve(lambda e, gq=gq: e.tensor_copy(sb_cmb[:, gq * 128:(gq + 1) * 128],
                                             pcmb),
              reads=[K], writes=["cmb"])



def emit_ffn(nc, p, h, st, kind, w, x_ap, y_ap, pfx, cache, cache_mode):
    E = 8
    FF = 4096 if kind == "l0" else 2048
    FFC = FF // 128
    alloc_gate(nc, st, h, pfx)
    sbt = lambda n, shp, dt: st.enter_context(nc.sbuf_tensor(pfx + n, shp, dt))
    sb_x = sbt("x", [128, DC, TT], F32)
    sb_xlf = sbt("xlf", [128, DC, TT], F32)
    sb_xlb = sbt("xlb", [128, DC, TT], BF16)
    sb_mid = sbt("mid", [128, FFC, TT], BF16)
    sb_yacc = sbt("yacc", [128, DC, TT], F32)
    sb_wstg = sbt("wstg", [128, 8, TT], F32)
    sb_w1g = [sbt("w1g0", [128, 8, TT], BF16), sbt("w1g1", [128, 8, TT], BF16)]
    if kind == "l1":
        sb_sg = sbt("sg", [128, TT], F32)
    sb_w2g = [sbt("w2g0", [128, FFC // 2, TT], BF16),
              sbt("w2g1", [128, FFC // 2, TT], BF16)]
    sb_gw = sbt("gw", [128, DC, E], F32)
    sb_cmb = sbt("cmb", [E, TT], F32)
    sb_cb = sbt("cb", [128, TT], F32)
    sb_logT = sbt("logT", [E, TT], F32)
    sb_tmp = [sbt("tmp0", [128, TT], F32), sbt("tmp1", [128, TT], F32)]
    if kind == "l0":
        sb_gb = sbt("gb", [E, 1], F32)
        sb_b1 = sbt("b1", [128, FFC, E], F32)
        sb_b2 = sbt("b2", [E, D], F32)
    p_g = st.enter_context(nc.psum_tensor(pfx + "p_g", [E, TT], F32))
    p_mid = [st.enter_context(nc.psum_tensor(pfx + "p_mid0", [128, TT], F32)),
             st.enter_context(nc.psum_tensor(pfx + "p_mid1", [128, TT], F32))]
    p_out = [st.enter_context(nc.psum_tensor(pfx + "p_out0", [128, TT], F32)),
             st.enter_context(nc.psum_tensor(pfx + "p_out1", [128, TT], F32))]
    w1c, w2c = cache

    p.dma(lambda e: e.dma_start(out=sb_gw[:],
                                in_=w["gw"].rearrange("(c p) e -> p c e", p=128)),
          reads=[], writes=["gw"])
    if kind == "l0":
        p.dma(lambda e: e.dma_start(out=sb_gb[:], in_=w["gb"]),
              reads=[], writes=["gb"])
        p.dma(lambda e: e.dma_start(out=sb_b1[:],
                                    in_=w["b1t"].rearrange("(c p) e -> p c e", p=128)),
              reads=[], writes=["b1"])
        p.dma(lambda e: e.dma_start(out=sb_b2[:], in_=w["b2"]),
              reads=[], writes=["b2"])
    w1_d, w2_d = w["w1"], w["w2"]

    def w1_load(t, ex, cg, buf):
        kb = f"w1g{buf}"
        kc = f"wc1:{ex}:{cg}"
        if cache_mode == "fill" and t == 0:
            p.dma(lambda e, ex=ex, cg=cg: e.dma_start(
                out=sb_wstg[:],
                in_=w1_d[ex, :, cg * TT:(cg + 1) * TT]
                    .rearrange("(c p) m -> p c m", p=128)),
                reads=[], writes=["wstg"])
            p.dve(lambda e, buf=buf: e.tensor_copy(sb_w1g[buf][:], sb_wstg[:]),
                  reads=["wstg"], writes=[kb])
            p.dma2(lambda e, ex=ex, cg=cg, buf=buf: e.dma_start(
                out=w1c[ex, :, cg * TT:(cg + 1) * TT]
                    .rearrange("(c p) m -> p c m", p=128),
                in_=sb_w1g[buf][:]),
                reads=[kb], writes=[kc])
        else:
            p.dma(lambda e, ex=ex, cg=cg, buf=buf: e.dma_start(
                out=sb_w1g[buf][:],
                in_=w1c[ex, :, cg * TT:(cg + 1) * TT]
                    .rearrange("(c p) m -> p c m", p=128)),
                reads=[kc], writes=[kb])

    for t in range(NTILE):
        t0 = t * TT
        p.dma(lambda e, t0=t0: e.dma_start(
            out=sb_x[:], in_=x_ap[:, t0:t0 + TT].rearrange("(c p) m -> p c m", p=128)),
            reads=[f"X:{t}"], writes=["x"])
        emit_ln(p, h, sb_x, sb_xlf, sb_xlb)
        for c in range(DC):
            p.mm(lambda e, c=c: e.matmul(p_g[:], sb_gw[:, c, :], sb_xlf[:, c, :],
                                         start=(c == 0), stop=(c == DC - 1)),
                 reads=["gw", "xlf", "logT"], writes=["p_g"])
        if kind == "l0":
            p.dve(lambda e: e.tensor_scalar(sb_logT[:], p_g[:], sb_gb[:],
                                            None, ALU.add),
                  reads=["p_g", "gb"], writes=["logT"])
        else:
            p.act(lambda e: e.activation(sb_logT[:], p_g[:], AF.Copy),
                  reads=["p_g"], writes=["logT"])
        emit_top2(p, h, sb_logT, sb_cmb, use_router_eps=(kind == "l1"))
        if kind == "l0":
            for fc in range(DC):
                b = fc % 2
                p.mm(lambda e, fc=fc, b=b: e.matmul(
                    p_out[b][:], sb_b2[:, fc * 128:(fc + 1) * 128], sb_cmb[:],
                    start=True, stop=True),
                    reads=["b2", "cmb"], writes=[f"po{b}"])
                p.act(lambda e, fc=fc, b=b: e.activation(sb_yacc[:, fc, :],
                                                         p_out[b][:], AF.Copy),
                      reads=[f"po{b}"], writes=[f"yacc{fc}"])
        else:
            p.dve(lambda e: e.memset(sb_yacc[:], 0.0),
                  reads=[], writes=[f"yacc{fc}" for fc in range(DC)])

        for ex in range(E):
            p.mm(lambda e, ex=ex: e.matmul(
                h["p_bc"][:], h["sel"][0:8, ex * 128:(ex + 1) * 128],
                sb_cmb[:], start=True, stop=True),
                reads=["sel", "cmb", "const"], writes=["p_bc"])
            p.act(lambda e: e.activation(sb_cb[:], h["p_bc"][:], AF.Copy),
                  reads=["p_bc"], writes=["cb"])
            if kind == "l0":
                for cg in range(FF // TT):
                    buf = cg % 2
                    w1_load(t, ex, cg, buf)
                    for fl in range(TT // 128):
                        fc = cg * (TT // 128) + fl
                        mb = fc % 2
                        for k in range(DC):
                            p.mm(lambda e, k=k, fl=fl, buf=buf, mb=mb: e.matmul(
                                p_mid[mb][:], sb_w1g[buf][:, k, fl * 128:(fl + 1) * 128],
                                sb_xlb[:, k, :], start=(k == 0), stop=(k == DC - 1)),
                                reads=[f"w1g{buf}", "xlb"], writes=[f"pm{mb}"])
                        p.act(lambda e, fc=fc, ex=ex, mb=mb: e.activation(
                            sb_mid[:, fc, :], p_mid[mb][:], AF.Relu,
                            bias=sb_b1[:, fc, ex:ex + 1]),
                            reads=[f"pm{mb}", "b1"], writes=[f"mid{fc}"])
            else:
                for cg in range(FF // TT):
                    w1_load(t, ex, cg, 0)
                    w1_load(t, ex, (FF // TT) + cg, 1)
                    for fl in range(TT // 128):
                        fc = cg * (TT // 128) + fl
                        for k in range(DC):
                            p.mm(lambda e, k=k, fl=fl: e.matmul(
                                p_mid[0][:], sb_w1g[0][:, k, fl * 128:(fl + 1) * 128],
                                sb_xlb[:, k, :], start=(k == 0), stop=(k == DC - 1)),
                                reads=["w1g0", "xlb"], writes=["pm0"])
                        p.act(lambda e: e.activation(sb_sg[:], p_mid[0][:], AF.Silu),
                              reads=["pm0"], writes=["sg"])
                        for k in range(DC):
                            p.mm(lambda e, k=k, fl=fl: e.matmul(
                                p_mid[1][:], sb_w1g[1][:, k, fl * 128:(fl + 1) * 128],
                                sb_xlb[:, k, :], start=(k == 0), stop=(k == DC - 1)),
                                reads=["w1g1", "xlb"], writes=["pm1"])
                        p.dve(lambda e, fc=fc: e.tensor_tensor(
                            sb_mid[:, fc, :], p_mid[1][:], sb_sg[:], ALU.mult),
                            reads=["pm1", "sg"], writes=[f"mid{fc}"])
            for cg2 in range(D // TT):
                for hf in range(2):
                    kb = f"w2g{hf}"
                    kc = f"wc2:{ex}:{cg2}:{hf}"
                    r0 = hf * (FF // 2)
                    if cache_mode == "fill" and t == 0:
                        for q in range(FFC // 16):
                            p.dma(lambda e, ex=ex, cg2=cg2, r0=r0, q=q: e.dma_start(
                                out=sb_wstg[:],
                                in_=w2_d[ex, r0 + q * 1024:r0 + (q + 1) * 1024,
                                         cg2 * TT:(cg2 + 1) * TT]
                                    .rearrange("(c p) m -> p c m", p=128)),
                                reads=[], writes=["wstg"])
                            p.dve(lambda e, hf=hf, q=q: e.tensor_copy(
                                sb_w2g[hf][:, q * 8:(q + 1) * 8, :], sb_wstg[:]),
                                reads=["wstg"], writes=[kb])
                        p.dma2(lambda e, ex=ex, cg2=cg2, r0=r0, hf=hf: e.dma_start(
                            out=w2c[ex, r0:r0 + FF // 2, cg2 * TT:(cg2 + 1) * TT]
                                .rearrange("(c p) m -> p c m", p=128),
                            in_=sb_w2g[hf][:]),
                            reads=[kb], writes=[kc])
                    else:
                        p.dma(lambda e, ex=ex, cg2=cg2, r0=r0, hf=hf: e.dma_start(
                            out=sb_w2g[hf][:],
                            in_=w2c[ex, r0:r0 + FF // 2, cg2 * TT:(cg2 + 1) * TT]
                                .rearrange("(c p) m -> p c m", p=128)),
                            reads=[kc], writes=[kb])
                for fl in range(TT // 128):
                    fc = cg2 * (TT // 128) + fl
                    b = fc % 2
                    for k in range(FFC):
                        hf = k // (FFC // 2)
                        kk = k % (FFC // 2)
                        p.mm(lambda e, kk=kk, hf=hf, fl=fl, b=b, k=k: e.matmul(
                            p_out[b][:], sb_w2g[hf][:, kk, fl * 128:(fl + 1) * 128],
                            sb_mid[:, k, :],
                            start=(k == 0), stop=(k == FFC - 1)),
                            reads=[f"w2g{hf}", f"mid{k}"], writes=[f"po{b}"])
                    p.dve(lambda e, b=b: e.tensor_tensor(
                        sb_tmp[b][:], p_out[b][:], sb_cb[:], ALU.mult),
                        reads=[f"po{b}", "cb"], writes=[f"tmp{b}"])
                    p.pool(lambda e, fc=fc, b=b: e.tensor_tensor(
                        sb_yacc[:, fc, :], sb_yacc[:, fc, :], sb_tmp[b][:], ALU.add),
                        reads=[f"tmp{b}", f"yacc{fc}"], writes=[f"yacc{fc}"])
        for c in range(DC):
            p.dve(lambda e, c=c: e.tensor_tensor(sb_x[:, c, :], sb_x[:, c, :],
                                                 sb_yacc[:, c, :], ALU.add),
                  reads=[f"yacc{c}", "x"], writes=["x"])
        p.dma2(lambda e, t0=t0: e.dma_start(
            out=y_ap[:, t0:t0 + TT].rearrange("(c p) m -> p c m", p=128),
            in_=sb_x[:]),
            reads=["x"], writes=[f"Y:{t}"])


def declare_ffn_weights(nc, kind, pfx):
    E, FF = 8, (4096 if kind == "l0" else 2048)
    w = {}
    w["gw"] = nc.dram_tensor(pfx + "gw", [D, E], F32, kind="ExternalInput").ap()
    if kind == "l0":
        w["gb"] = nc.dram_tensor(pfx + "gb", [E, 1], F32, kind="ExternalInput").ap()
        w["w1"] = nc.dram_tensor(pfx + "w1", [E, D, FF], F32, kind="ExternalInput").ap()
        w["b1t"] = nc.dram_tensor(pfx + "b1t", [FF, E], F32, kind="ExternalInput").ap()
        w["w2"] = nc.dram_tensor(pfx + "w2", [E, FF, D], F32, kind="ExternalInput").ap()
        w["b2"] = nc.dram_tensor(pfx + "b2", [E, D], F32, kind="ExternalInput").ap()
    else:
        w["w1"] = nc.dram_tensor(pfx + "w13", [E, D, 2 * FF], F32,
                                 kind="ExternalInput").ap()
        w["w2"] = nc.dram_tensor(pfx + "w2", [E, FF, D], F32, kind="ExternalInput").ap()
    return w


def ffn_weight_map(kind, pfx, ins):
    m = {}
    if kind == "l0":
        m[pfx + "gw"] = f32(ins["L0_ffn_gate_w"])
        m[pfx + "gb"] = f32(np.asarray(ins["L0_ffn_gate_b"]).reshape(8, 1))
        m[pfx + "w1"] = f32(ins["L0_ffn_w1"])
        m[pfx + "b1t"] = f32(np.asarray(ins["L0_ffn_b1"]).T)
        m[pfx + "w2"] = f32(ins["L0_ffn_w2"])
        m[pfx + "b2"] = f32(ins["L0_ffn_b2"])
    else:
        m[pfx + "gw"] = f32(np.asarray(ins["L1_ffn_router_w"]).T)
        m[pfx + "w13"] = f32(ins["L1_ffn_w13"])
        m[pfx + "w2"] = f32(ins["L1_ffn_w2"])
    return m



def emit_l0_attn(nc, p, h, st, w, x_ap, y_ap):
    NH = 16
    q_i = nc.dram_tensor("at_q_i", [D, T], BF16, kind="Internal").ap()
    k_i = nc.dram_tensor("at_k_i", [D, T], BF16, kind="Internal").ap()
    v_i = nc.dram_tensor("at_v_i", [T, NH * 65], BF16, kind="Internal").ap()
    xr_i = nc.dram_tensor("at_xr_i", [D, T], F32, kind="Internal").ap()
    a_i = nc.dram_tensor("at_a_i", [D, T], BF16, kind="Internal").ap()

    sbt = lambda n, shp, dt: st.enter_context(nc.sbuf_tensor("at" + n, shp, dt))
    sb_x = sbt("x", [128, DC, TT], F32)
    sb_xlf = sbt("xlf", [128, DC, TT], F32)
    sb_xlb = sbt("xlb", [128, DC, TT], BF16)
    sb_xr = sbt("xr", [128, TT], F32)
    sb_wstg = sbt("wstg", [128, 8, TT], F32)
    sb_wb = sbt("wb", [128, DC, D], BF16)
    sb_proj = sbt("proj", [128, DC, TT], BF16)
    sb_v65 = sbt("v65", [128, NH * 65], BF16)
    sb_kh = sbt("kh", [64, T], BF16)
    sb_vh = sbt("vh", [128, T // 128, 65], BF16)
    sb_qh = sbt("qh", [64, TT], BF16)
    sb_S = sbt("S", [128, T // 128, TT], BF16)
    sb_m = sbt("m", [128, TT], F32)
    sb_cmx = sbt("cmx", [128, 1], F32)
    sb_cmr = sbt("cmr", [1, TT], F32)
    sb_nmb = sbt("nmb", [128, TT], F32)
    sb_esc = sbt("esc", [128, TT], F32)
    sb_inv = sbt("inv", [1, TT], F32)
    sb_invb = sbt("invb", [128, TT], F32)
    sb_an = sbt("an", [64, TT], BF16)
    sb_Ag = sbt("Ag", [128, DC, TT], BF16)
    sb_out = sbt("out", [128, TT], F32)
    p_tm = st.enter_context(nc.psum_tensor("at_p_tm", [128, 128], F32))
    p_s = st.enter_context(nc.psum_tensor("at_p_s", [128, TT], F32))
    p_o = st.enter_context(nc.psum_tensor("at_p_o", [65, TT], F32))
    p_mm = st.enter_context(nc.psum_tensor("at_p_mm", [128, TT], F32))

    def load_weight(w_d):
        for half in range(2):
            p.dma(lambda e, half=half: e.dma_start(
                out=sb_wstg[:],
                in_=w_d[:, half * TT:(half + 1) * TT]
                    .rearrange("(c p) m -> p c m", p=128)))
            p.dve(lambda e, half=half: e.tensor_copy(
                sb_wb[:, :, half * TT:(half + 1) * TT], sb_wstg[:]))

    for t in range(NTILE):
        t0 = t * TT
        p.dma(lambda e, t0=t0: e.dma_start(
            out=sb_x[:], in_=x_ap[:, t0:t0 + TT].rearrange("(c p) m -> p c m", p=128)))
        emit_ln(p, h, sb_x, sb_xlf, sb_xlb)
        for c in range(DC):
            p.dve(lambda e, c=c: e.tensor_tensor(sb_xr[:], sb_x[:, c, :],
                                                 sb_xlf[:, c, :], ALU.add))
            p.dma(lambda e, c=c, t0=t0: e.dma_start(
                out=xr_i[c * 128:(c + 1) * 128, t0:t0 + TT], in_=sb_xr[:]))
        for w_d, dst, scale in ((w["wq"], q_i, 0.125), (w["wk"], k_i, 1.0)):
            load_weight(w_d)
            for mc in range(DC):
                for k in range(DC):
                    p.mm(lambda e, mc=mc, k=k: e.matmul(
                        p_mm[:], sb_wb[:, k, mc * 128:(mc + 1) * 128],
                        sb_xlb[:, k, :], start=(k == 0), stop=(k == DC - 1)))
                p.act(lambda e, mc=mc, scale=scale: e.activation(
                    sb_proj[:, mc, :], p_mm[:], AF.Copy, scale=scale))
            p.dma(lambda e, dst=dst, t0=t0: e.dma_start(
                out=dst[:, t0:t0 + TT].rearrange("(c p) m -> p c m", p=128),
                in_=sb_proj[:]))
        load_weight(w["wv"])
        for tc in range(TT // 128):
            for half in range(2):
                for k in range(DC):
                    p.mm(lambda e, tc=tc, half=half, k=k: e.matmul(
                        p_mm[:], sb_xlb[:, k, tc * 128:(tc + 1) * 128],
                        sb_wb[:, k, half * TT:(half + 1) * TT],
                        start=(k == 0), stop=(k == DC - 1)))
                p.dve(lambda e, half=half: e.tensor_copy(
                    sb_v65[:, half * 520:(half + 1) * 520]
                        .rearrange("p (hh q) -> p hh q", q=65)[:, :, 0:64],
                    p_mm[:].rearrange("p (hh q) -> p hh q", q=64)))
            p.dve(lambda e: e.memset(
                sb_v65[:].rearrange("p (hh q) -> p hh q", q=65)[:, :, 64:65], 1.0))
            p.dma(lambda e, t0=t0, tc=tc: e.dma_start(
                out=v_i[t0 + tc * 128:t0 + (tc + 1) * 128, :], in_=sb_v65[:]))

    NTC = T // 128
    for hh in range(NH):
        p.dma(lambda e, hh=hh: e.dma_start(
            out=sb_kh[:], in_=k_i[hh * 64:(hh + 1) * 64, :]))
        p.dma(lambda e, hh=hh: e.dma_start(
            out=sb_vh[:], in_=v_i[:, hh * 65:(hh + 1) * 65]
                .rearrange("(c p) n -> p c n", p=128)))
        for grp in range(NTILE):
            g0 = grp * TT
            p.dma(lambda e, hh=hh, g0=g0: e.dma_start(
                out=sb_qh[:], in_=q_i[hh * 64:(hh + 1) * 64, g0:g0 + TT]))
            for tc in range(NTC):
                p.mm(lambda e, tc=tc: e.matmul(
                    p_s[:], sb_kh[:, tc * 128:(tc + 1) * 128], sb_qh[:],
                    start=True, stop=True))
                p.act(lambda e, tc=tc: e.activation(sb_S[:, tc, :], p_s[:], AF.Copy))
                if tc == 0:
                    p.dve(lambda e: e.tensor_copy(sb_m[:], p_s[:]))
                else:
                    p.dve(lambda e: e.tensor_tensor(sb_m[:], sb_m[:], p_s[:], ALU.max))
            for b in range(TT // 128):
                p.mm(lambda e, b=b: e.transpose(
                    p_tm[:], sb_m[:, b * 128:(b + 1) * 128], h["idf"][:]))
                p.dve(lambda e: e.tensor_reduce(sb_cmx[:], p_tm[:], AX, ALU.max))
                p.mm(lambda e: e.transpose(p_tm[0:1, 0:128], sb_cmx[:], h["idf"][:]))
                p.dve(lambda e, b=b: e.tensor_copy(
                    sb_cmr[:, b * 128:(b + 1) * 128], p_tm[0:1, 0:128]))
            p.mm(lambda e: e.matmul(h["p_bc"][:], h["ones_row"][:], sb_cmr[:],
                                    start=True, stop=True))
            p.act(lambda e: e.activation(sb_nmb[:], h["p_bc"][:], AF.Copy, scale=-1.0))
            for tc in range(NTC):
                p.dve(lambda e, tc=tc: e.tensor_tensor(
                    sb_esc[:], sb_S[:, tc, :], sb_nmb[:], ALU.add))
                p.act(lambda e, tc=tc: e.activation(sb_S[:, tc, :], sb_esc[:], AF.Exp))
            for tc in range(NTC):
                p.mm(lambda e, tc=tc: e.matmul(
                    p_o[:], sb_vh[:, tc, :], sb_S[:, tc, :],
                    start=(tc == 0), stop=(tc == NTC - 1)))
            p.dve(lambda e: e.reciprocal(sb_inv[:], p_o[64:65, :]))
            p.mm(lambda e: e.matmul(h["p_bc"][:], h["ones_row"][:], sb_inv[:],
                                    start=True, stop=True))
            p.act(lambda e: e.activation(sb_invb[:], h["p_bc"][:], AF.Copy))
            p.dve(lambda e: e.tensor_tensor(
                sb_an[:], p_o[0:64, :], sb_invb[0:64, :], ALU.mult))
            p.dma(lambda e, hh=hh, g0=g0: e.dma_start(
                out=a_i[hh * 64:(hh + 1) * 64, g0:g0 + TT], in_=sb_an[:]))

    load_weight(w["wp"])
    for grp in range(NTILE):
        g0 = grp * TT
        p.dma(lambda e, g0=g0: e.dma_start(
            out=sb_proj[:], in_=a_i[:, g0:g0 + TT].rearrange("(c p) m -> p c m", p=128)))
        for c in range(DC):
            p.act(lambda e, c=c: e.activation(
                sb_Ag[:, c, :], sb_proj[:, c, :], AF.Gelu))
        for mc in range(DC):
            for k in range(DC):
                p.mm(lambda e, mc=mc, k=k: e.matmul(
                    p_mm[:], sb_wb[:, k, mc * 128:(mc + 1) * 128],
                    sb_Ag[:, k, :], start=(k == 0), stop=(k == DC - 1)))
            p.dma(lambda e, mc=mc, g0=g0: e.dma_start(
                out=sb_xr[:], in_=xr_i[mc * 128:(mc + 1) * 128, g0:g0 + TT]))
            p.dve(lambda e: e.tensor_tensor(sb_out[:], p_mm[:], sb_xr[:], ALU.add))
            p.dma(lambda e, mc=mc, g0=g0: e.dma_start(
                out=y_ap[mc * 128:(mc + 1) * 128, g0:g0 + TT], in_=sb_out[:]))


def declare_l0_attn_weights(nc):
    w = {}
    for n in ("wq", "wk", "wv", "wp"):
        w[n] = nc.dram_tensor("at_" + n, [D, D], F32, kind="ExternalInput").ap()
    return w


def l0_attn_weight_map(ins):
    return {"at_wq": f32(ins["L0_attn_Wq"]), "at_wk": f32(ins["L0_attn_Wk"]),
            "at_wv": f32(ins["L0_attn_Wv"]), "at_wp": f32(ins["L0_attn_Wpost"])}



def emit_l1_attn(nc, p, h, st, w, x_ap, y_ap):
    DI, DS = 2048, 16
    DIC = DI // 128
    TC = 512
    NCH = T // TC
    u_i = nc.dram_tensor("ma_u_i", [DI, T], BF16, kind="Internal").ap()
    sr_i = nc.dram_tensor("ma_sr_i", [DI, T], BF16, kind="Internal").ap()
    dl_i = nc.dram_tensor("ma_dl_i", [DI, T], F32, kind="Internal").ap()
    du_i = nc.dram_tensor("ma_du_i", [DI, T], BF16, kind="Internal").ap()
    bt_i = nc.dram_tensor("ma_bt_i", [DS, T], F32, kind="Internal").ap()
    ct_i = nc.dram_tensor("ma_ct_i", [DS, T], F32, kind="Internal").ap()

    sbt = lambda n, shp, dt: st.enter_context(nc.sbuf_tensor("ma" + n, shp, dt))
    sb_x = sbt("x", [128, DC, TT], F32)
    sb_xlb = sbt("xlb", [128, DC, TT], BF16)
    sb_u = sbt("u", [128, DIC, TT], BF16)
    sb_wstg = sbt("wstg", [128, 8, TT], F32)
    sb_wd = sbt("wd", [128, DIC, TT], BF16)
    sb_wbc = sbt("wbc", [128, DIC, 2 * DS], BF16)
    sb_wbcf = sbt("wbcf", [128, DIC, 2 * DS], F32)
    sb_ch1 = sbt("ch1", [128, TT], F32)
    sb_ch2 = sbt("ch2", [128, TT], F32)
    sb_chb = sbt("chb", [128, TT], BF16)
    sb_bc = sbt("bc", [DS, TT], F32)
    sb_A = sbt("A", [128, DIC * DS], F32)
    sb_bd = sbt("bd", [128, DIC], F32)
    sb_dp = sbt("dp", [128, DIC], F32)
    sb_dl = sbt("dl", [128, DIC, TC], F32)
    sb_btc = sbt("btc", [DS, TC], F32)
    sb_ctc = sbt("ctc", [DS, TC], F32)
    sb_brep = sbt("brep", [128, TC], BF16)
    sb_crep = sbt("crep", [128, TC], BF16)
    sb_a = sbt("a", [128, TC], BF16)
    sb_b = sbt("b", [128, TC], BF16)
    sb_htr = sbt("htr", [128, TC], F32)
    sb_yc = sbt("yc", [128, TC], F32)
    sb_hst = sbt("hst", [128, DIC * DS], F32)
    sb_y = sbt("y", [128, DIC, TC], F32)
    sb_sr = sbt("sr", [128, DIC, TC], BF16)
    sb_o = sbt("o", [128, TT], F32)
    p_mm = st.enter_context(nc.psum_tensor("ma_p_mm", [128, TT], F32))
    p_bc2 = st.enter_context(nc.psum_tensor("ma_p_bc2", [128, TC], F32))
    p_sm = st.enter_context(nc.psum_tensor("ma_p_sm", [DS, TT], F32))

    p.dma(lambda e: e.dma_start(out=sb_bd[:], in_=w["bdl"]))
    p.dma(lambda e: e.dma_start(out=sb_dp[:], in_=w["dpar"]))
    p.dma(lambda e: e.dma_start(
        out=sb_A[:].rearrange("p (c n) -> p c n", n=DS),
        in_=w["alog"].rearrange("(c p) n -> p c n", p=128)))
    p.act(lambda e: e.activation(sb_A[:], sb_A[:], AF.Exp))
    p.dve(lambda e: e.tensor_scalar(sb_A[:], sb_A[:], -1.0, None, ALU.mult))
    p.dma(lambda e: e.dma_start(
        out=sb_wbcf[:, :, 0:DS], in_=w["wB"].rearrange("(c p) n -> p c n", p=128)))
    p.dma(lambda e: e.dma_start(
        out=sb_wbcf[:, :, DS:2 * DS], in_=w["wC"].rearrange("(c p) n -> p c n", p=128)))
    p.dve(lambda e: e.tensor_copy(sb_wbc[:], sb_wbcf[:]))

    def stream_w(w_ap, rows, cols, dst, dst_k0):
        p.dma(lambda e: e.dma_start(
            out=sb_wstg[:], in_=w_ap[rows[0]:rows[1], cols[0]:cols[1]]
                .rearrange("(c p) m -> p c m", p=128)))
        p.dve(lambda e: e.tensor_copy(dst[:, dst_k0:dst_k0 + 8, :], sb_wstg[:]))

    for t in range(NTILE):
        t0 = t * TT
        p.dma(lambda e, t0=t0: e.dma_start(
            out=sb_x[:], in_=x_ap[:, t0:t0 + TT].rearrange("(c p) m -> p c m", p=128)))
        emit_ln(p, h, sb_x, None, sb_xlb)
        for cg in range(DI // TT):
            stream_w(w["win"], (0, D), (cg * TT, (cg + 1) * TT), sb_wd, 0)
            for fl in range(TT // 128):
                mc = cg * (TT // 128) + fl
                for k in range(DC):
                    p.mm(lambda e, k=k, fl=fl: e.matmul(
                        p_mm[:], sb_wd[:, k, fl * 128:(fl + 1) * 128],
                        sb_xlb[:, k, :], start=(k == 0), stop=(k == DC - 1)))
                p.act(lambda e, mc=mc: e.activation(sb_u[:, mc, :], p_mm[:], AF.Silu))
        p.dma(lambda e, t0=t0: e.dma_start(
            out=u_i[:, t0:t0 + TT].rearrange("(c p) m -> p c m", p=128), in_=sb_u[:]))
        for cg in range(DI // TT):
            stream_w(w["win"], (0, D), (DI + cg * TT, DI + (cg + 1) * TT), sb_wd, 0)
            for fl in range(TT // 128):
                mc = cg * (TT // 128) + fl
                for k in range(DC):
                    p.mm(lambda e, k=k, fl=fl: e.matmul(
                        p_mm[:], sb_wd[:, k, fl * 128:(fl + 1) * 128],
                        sb_xlb[:, k, :], start=(k == 0), stop=(k == DC - 1)))
                p.act(lambda e: e.activation(sb_chb[:], p_mm[:], AF.Silu))
                p.dma(lambda e, mc=mc, t0=t0: e.dma_start(
                    out=sr_i[mc * 128:(mc + 1) * 128, t0:t0 + TT], in_=sb_chb[:]))
        for cg in range(DI // TT):
            stream_w(w["wdl"], (0, 1024), (cg * TT, (cg + 1) * TT), sb_wd, 0)
            stream_w(w["wdl"], (1024, 2048), (cg * TT, (cg + 1) * TT), sb_wd, 8)
            for fl in range(TT // 128):
                mc = cg * (TT // 128) + fl
                for k in range(DIC):
                    p.mm(lambda e, k=k, fl=fl: e.matmul(
                        p_mm[:], sb_wd[:, k, fl * 128:(fl + 1) * 128],
                        sb_u[:, k, :], start=(k == 0), stop=(k == DIC - 1)))
                p.act(lambda e, mc=mc: e.activation(
                    sb_ch1[:], p_mm[:], AF.Relu, bias=sb_bd[:, mc:mc + 1]))
                p.act(lambda e, mc=mc: e.activation(
                    sb_ch2[:], p_mm[:], AF.Abs, bias=sb_bd[:, mc:mc + 1]))
                p.act(lambda e: e.activation(sb_ch2[:], sb_ch2[:], AF.Exp, scale=-1.0))
                p.act(lambda e: e.activation(sb_ch2[:], sb_ch2[:], AF.Ln,
                                             bias=h["cst"][:, 0:1]))
                p.dve(lambda e: e.tensor_tensor(sb_ch1[:], sb_ch1[:], sb_ch2[:],
                                                ALU.add))
                p.dma(lambda e, mc=mc, t0=t0: e.dma_start(
                    out=dl_i[mc * 128:(mc + 1) * 128, t0:t0 + TT], in_=sb_ch1[:]))
                p.dve(lambda e, mc=mc: e.tensor_tensor(
                    sb_chb[:], sb_ch1[:], sb_u[:, mc, :], ALU.mult))
                p.dma(lambda e, mc=mc, t0=t0: e.dma_start(
                    out=du_i[mc * 128:(mc + 1) * 128, t0:t0 + TT], in_=sb_chb[:]))
        for which, dst in ((0, bt_i), (1, ct_i)):
            for k in range(DIC):
                p.mm(lambda e, k=k, which=which: e.matmul(
                    p_sm[:], sb_wbc[:, k, which * DS:(which + 1) * DS],
                    sb_u[:, k, :], start=(k == 0), stop=(k == DIC - 1)))
            p.act(lambda e: e.activation(sb_bc[:], p_sm[:], AF.Copy))
            p.dma(lambda e, dst=dst, t0=t0: e.dma_start(
                out=dst[:, t0:t0 + TT], in_=sb_bc[:]))

    p.dve(lambda e: e.memset(sb_hst[:], 0.0))
    for ch in range(NCH):
        c0 = ch * TC
        p.dma(lambda e, c0=c0: e.dma_start(
            out=sb_dl[:], in_=dl_i[:, c0:c0 + TC].rearrange("(c p) m -> p c m", p=128)))
        p.dma(lambda e, c0=c0: e.dma_start(
            out=sb_u[:, :, 0:TC], in_=du_i[:, c0:c0 + TC]
                .rearrange("(c p) m -> p c m", p=128)))
        p.dma(lambda e, c0=c0: e.dma_start(out=sb_btc[:], in_=bt_i[:, c0:c0 + TC]))
        p.dma(lambda e, c0=c0: e.dma_start(out=sb_ctc[:], in_=ct_i[:, c0:c0 + TC]))
        p.dve(lambda e: e.memset(sb_y[:], 0.0))
        for n in range(DS):
            p.mm(lambda e, n=n: e.matmul(
                p_bc2[:], h["sel"][:, n * 128:(n + 1) * 128], sb_btc[:],
                start=True, stop=True))
            p.act(lambda e: e.activation(sb_brep[:], p_bc2[:], AF.Copy))
            p.mm(lambda e, n=n: e.matmul(
                p_bc2[:], h["sel"][:, n * 128:(n + 1) * 128], sb_ctc[:],
                start=True, stop=True))
            p.act(lambda e: e.activation(sb_crep[:], p_bc2[:], AF.Copy))
            for dh in range(DIC):
                col = dh * DS + n
                p.act(lambda e, dh=dh, col=col: e.activation(
                    sb_a[:], sb_dl[:, dh, :], AF.Exp, scale=sb_A[:, col:col + 1]))
                p.dve(lambda e, dh=dh: e.tensor_tensor(
                    sb_b[:], sb_u[:, dh, 0:TC], sb_brep[:], ALU.mult))
                p.dve(lambda e, col=col: e.tensor_tensor_scan(
                    sb_htr[:], sb_a[:], sb_b[:], sb_hst[:, col:col + 1],
                    ALU.mult, ALU.add))
                p.dve(lambda e, col=col: e.tensor_copy(
                    sb_hst[:, col:col + 1], sb_htr[:, TC - 1:TC]))
                p.dve(lambda e: e.tensor_tensor(sb_yc[:], sb_htr[:], sb_crep[:],
                                                ALU.mult))
                p.pool(lambda e, dh=dh: e.tensor_tensor(
                    sb_y[:, dh, :], sb_y[:, dh, :], sb_yc[:], ALU.add))
        p.dma(lambda e, c0=c0: e.dma_start(
            out=sb_u[:, :, 0:TC], in_=u_i[:, c0:c0 + TC]
                .rearrange("(c p) m -> p c m", p=128)))
        p.dma(lambda e, c0=c0: e.dma_start(
            out=sb_sr[:], in_=sr_i[:, c0:c0 + TC].rearrange("(c p) m -> p c m", p=128)))
        p.dma(lambda e, c0=c0: e.dma_start(
            out=sb_x[:, :, 0:TC], in_=x_ap[:, c0:c0 + TC]
                .rearrange("(c p) m -> p c m", p=128)))
        for dh in range(DIC):
            p.dve(lambda e, dh=dh: e.scalar_tensor_tensor(
                sb_y[:, dh, :], sb_u[:, dh, 0:TC], sb_dp[:, dh:dh + 1],
                sb_y[:, dh, :], ALU.mult, ALU.add))
            p.dve(lambda e, dh=dh: e.tensor_tensor(
                sb_sr[:, dh, :], sb_y[:, dh, :], sb_sr[:, dh, :], ALU.mult))
        for cg in range(D // TT):
            stream_w(w["wout"], (0, 1024), (cg * TT, (cg + 1) * TT), sb_wd, 0)
            stream_w(w["wout"], (1024, 2048), (cg * TT, (cg + 1) * TT), sb_wd, 8)
            for fl in range(TT // 128):
                mc = cg * (TT // 128) + fl
                for k in range(DIC):
                    p.mm(lambda e, k=k, fl=fl: e.matmul(
                        p_mm[0:128, 0:TC], sb_wd[:, k, fl * 128:(fl + 1) * 128],
                        sb_sr[:, k, :], start=(k == 0), stop=(k == DIC - 1)))
                p.dve(lambda e, mc=mc: e.tensor_tensor(
                    sb_o[:, 0:TC], p_mm[0:128, 0:TC], sb_x[:, mc, 0:TC], ALU.add))
                p.dma(lambda e, mc=mc, c0=c0: e.dma_start(
                    out=y_ap[mc * 128:(mc + 1) * 128, c0:c0 + TC], in_=sb_o[:, 0:TC]))


def declare_l1_attn_weights(nc):
    DI, DS = 2048, 16
    w = {}
    w["win"] = nc.dram_tensor("ma_win", [D, 2 * DI], F32, kind="ExternalInput").ap()
    w["wdl"] = nc.dram_tensor("ma_wdl", [DI, DI], F32, kind="ExternalInput").ap()
    w["bdl"] = nc.dram_tensor("ma_bdl", [128, DI // 128], F32, kind="ExternalInput").ap()
    w["wB"] = nc.dram_tensor("ma_wB", [DI, DS], F32, kind="ExternalInput").ap()
    w["wC"] = nc.dram_tensor("ma_wC", [DI, DS], F32, kind="ExternalInput").ap()
    w["alog"] = nc.dram_tensor("ma_alog", [DI, DS], F32, kind="ExternalInput").ap()
    w["dpar"] = nc.dram_tensor("ma_dpar", [128, DI // 128], F32, kind="ExternalInput").ap()
    w["wout"] = nc.dram_tensor("ma_wout", [DI, D], F32, kind="ExternalInput").ap()
    return w


def l1_attn_weight_map(ins):
    return {
        "ma_win": f32(ins["L1_attn_W_in"]),
        "ma_wdl": f32(ins["L1_attn_W_delta"]),
        "ma_bdl": f32(np.asarray(ins["L1_attn_b_delta"]).reshape(16, 128).T),
        "ma_wB": f32(ins["L1_attn_W_B"]),
        "ma_wC": f32(ins["L1_attn_W_C"]),
        "ma_alog": f32(ins["L1_attn_A_log"]),
        "ma_dpar": f32(np.asarray(ins["L1_attn_D_param"]).reshape(16, 128).T),
        "ma_wout": f32(ins["L1_attn_W_out"]),
    }



def build_fused():
    nc = bass.Bass("TRN2", target_bir_lowering=False)
    x_d = nc.dram_tensor("x_fm", [D, T], F32, kind="ExternalInput").ap()
    y_d = nc.dram_tensor("y_fm", [D, T], F32, kind="ExternalOutput").ap()
    xs = [nc.dram_tensor(f"xs{i}", [D, T], F32, kind="Internal").ap()
          for i in range(1, 6)]
    w_l0f = declare_ffn_weights(nc, "l0", "l0f_")
    w_at = declare_l0_attn_weights(nc)
    w_l1f = declare_ffn_weights(nc, "l1", "l1f_")
    w_ma = declare_l1_attn_weights(nc)
    w1c_l0 = nc.dram_tensor("w1c_l0", [8, D, 4096], BF16, kind="Internal").ap()
    w2c_l0 = nc.dram_tensor("w2c_l0", [8, 4096, D], BF16, kind="Internal").ap()
    w1c_l1 = nc.dram_tensor("w1c_l1", [8, D, 4096], BF16, kind="Internal").ap()
    w2c_l1 = nc.dram_tensor("w2c_l1", [8, 2048, D], BF16, kind="Internal").ap()

    with contextlib.ExitStack() as st:
        h = alloc_common(nc, st)
        sems = {e: st.enter_context(nc.semaphore("sem_" + e)) for e in ENGS}
        block = st.enter_context(nc.Block())
        p = Prog(block, sems)
        declare_common(nc, p, h)
        alloc_ln(nc, st, h)
        phases = [
            lambda pst: emit_ffn(nc, p, h, pst, "l0", w_l0f, x_d, xs[0], "ph0_",
                                 (w1c_l0, w2c_l0), "fill"),
            lambda pst: emit_l0_attn(nc, p, h, pst, w_at, xs[0], xs[1]),
            lambda pst: emit_ffn(nc, p, h, pst, "l0", w_l0f, xs[1], xs[2], "ph2_",
                                 (w1c_l0, w2c_l0), "use"),
            lambda pst: emit_ffn(nc, p, h, pst, "l1", w_l1f, xs[2], xs[3], "ph3_",
                                 (w1c_l1, w2c_l1), "fill"),
            lambda pst: emit_l1_attn(nc, p, h, pst, w_ma, xs[3], xs[4]),
            lambda pst: emit_ffn(nc, p, h, pst, "l1", w_l1f, xs[4], y_d, "ph5_",
                                 (w1c_l1, w2c_l1), "use"),
        ]
        for ph in phases:
            p.barrier(h)
            with contextlib.ExitStack() as pst:
                ph(pst)
                p.flush()
        p.finish()
    return nc


_prog_cache = {}


def _get_prog(name, builder):
    if name not in _prog_cache:
        _prog_cache[name] = builder()
    return _prog_cache[name]


def kernel(**inputs):
    x = np.asarray(inputs["L0_attn_X"], np.float32)
    shards = [f32(x[i].T) for i in range(N_CORES)]

    nc = _get_prog("fused", build_fused)
    wmap = dict(common_inputs())
    wmap.update(ffn_weight_map("l0", "l0f_", inputs))
    wmap.update(l0_attn_weight_map(inputs))
    wmap.update(ffn_weight_map("l1", "l1f_", inputs))
    wmap.update(l1_attn_weight_map(inputs))
    maps = []
    for s in shards:
        m = dict(wmap)
        m["x_fm"] = s
        maps.append(m)
    res = run_bass_kernel_spmd(nc, maps, core_ids=list(range(N_CORES)))
    out = np.stack([np.asarray(res.results[i]["y_fm"]).T for i in range(N_CORES)], 0)
    return np.ascontiguousarray(out, np.float32)


# version 04 (final)
# speedup vs baseline: 1.3292x; 1.3292x over previous
import contextlib

import numpy as np
import ml_dtypes
import concourse.bass as bass
import concourse.mybir as mybir
from concourse.bass_utils import run_bass_kernel_spmd

F32 = mybir.dt.float32
BF16 = mybir.dt.bfloat16
AF = mybir.ActivationFunctionType
ALU = mybir.AluOpType
AX = mybir.AxisListType.X

T = 4096
D = 1024
DC = D // 128
TT = 512
NTILE = T // TT
N_CORES = 8

f32 = lambda a: np.ascontiguousarray(a, np.float32)


ENGS = ("tensor", "scalar", "vector", "gpsimd", "sync", "scalar_dma")
NDMA = 16
CHANNELS = ("tensor", "scalar", "vector", "gpsimd") + tuple(
    f"dma{i}" for i in range(NDMA))


class Prog:

    PSUM_KEYS = frozenset([
        "p_s1", "p_s2", "p_bc", "p_g", "pm0", "pm1", "po0", "po1",
        "ps0", "ps1", "pmm0", "pmm1", "ptm", "po", "pbc2", "psm"])

    def __init__(self, block, sems):
        self.block = block
        self.sems = sems
        self.ops = []
        self.tot = {c: 0 for c in CHANNELS}
        self.last_write = {}
        self.readers = {}
        self.barrier_ms = {c: 0 for c in CHANNELS}
        self.streams = set()
        self.dma_rr = 0

    def op(self, stream, fn, reads=None, writes=(), inc=1):
        if inc == 16:
            chan = f"dma{self.dma_rr}"
            self.dma_rr = (self.dma_rr + 1) % NDMA
        else:
            chan = stream
        deps = dict(self.barrier_ms)
        if inc == 16:
            deps[chan] = max(deps[chan], self.tot[chan])
        if reads is None:
            for c in CHANNELS:
                deps[c] = max(deps[c], self.tot[c])
        else:
            for k in reads:
                d = self.last_write.get(k)
                if d:
                    deps[d[0]] = max(deps[d[0]], d[1])
            for k in writes:
                d = self.last_write.get(k)
                if d:
                    deps[d[0]] = max(deps[d[0]], d[1])
                for d in self.readers.get(k, ()):
                    deps[d[0]] = max(deps[d[0]], d[1])
        ms = self.tot[chan] + inc
        self.tot[chan] = ms
        self.ops.append((stream, chan, fn, inc, deps))
        if reads is None:
            for c in CHANNELS:
                self.barrier_ms[c] = self.tot[c]
            self.last_write.clear()
            self.readers.clear()
        else:
            me = (chan, ms)
            for k in reads:
                if k in self.PSUM_KEYS:
                    self.last_write[k] = me
                    self.readers[k] = []
                else:
                    self.readers.setdefault(k, []).append(me)
            for k in writes:
                self.last_write[k] = me
                self.readers[k] = []

    def mm(self, fn, reads=None, writes=()):
        self.op("tensor", fn, reads, writes)

    def act(self, fn, reads=None, writes=()):
        self.op("scalar", fn, reads, writes)

    def dve(self, fn, reads=None, writes=()):
        self.op("vector", fn, reads, writes)

    def pool(self, fn, reads=None, writes=()):
        self.op("gpsimd", fn, reads, writes)

    def dma(self, fn, reads=None, writes=()):
        self.op("sync", fn, reads, writes, inc=16)

    def dma2(self, fn, reads=None, writes=()):
        self.op("scalar_dma", fn, reads, writes, inc=16)

    def flush(self):
        stream_of = lambda s: "scalar" if s == "scalar_dma" else s
        by_stream = {}
        for item in self.ops:
            by_stream.setdefault(stream_of(item[0]), []).append(item)
        self.ops = []
        for stream, items in by_stream.items():
            self.streams.add(stream)
            dec = getattr(self.block, stream)

            def body(engine, items=items, stream=stream):
                waited = {c: 0 for c in CHANNELS}
                for (_st, chan, fn, inc, deps) in items:
                    for dc, dv in deps.items():
                        if dv <= waited[dc]:
                            continue
                        if dc == chan == "tensor":
                            continue
                        engine.wait_ge(self.sems[dc], dv)
                        waited[dc] = dv
                    fn(engine).then_inc(self.sems[chan], inc)

            dec(body)

    def barrier(self, h):
        self.op("vector", lambda e: e.memset(h["s_mean"][:], 0.0), reads=None)

    def finish(self):
        self.flush()
        for stream in self.streams:
            dec = getattr(self.block, stream)

            def fin(engine):
                for c in CHANNELS:
                    if self.tot[c]:
                        engine.wait_ge(self.sems[c], self.tot[c])

            dec(fin)



def alloc_common(nc, st):
    h = {}
    h["ones_col"] = st.enter_context(nc.sbuf_tensor("s_ones_col", [128, 1], F32))
    h["ones_row"] = st.enter_context(nc.sbuf_tensor("s_ones_row", [1, 128], F32))
    h["idf"] = st.enter_context(nc.sbuf_tensor("s_idf", [128, 128], F32))
    h["cst"] = st.enter_context(nc.sbuf_tensor("s_cst", [128, 8], F32))
    h["sel"] = st.enter_context(nc.sbuf_tensor("s_sel", [16, 16 * 128], F32))
    return h


def common_inputs():
    cst = np.zeros((128, 8), np.float32)
    cst[:, 0] = 1.0
    cst[:, 1] = 1e-5
    sel = np.zeros((16, 16 * 128), np.float32)
    for e in range(16):
        sel[e, e * 128:(e + 1) * 128] = 1.0
    return {
        "c_sel": sel,
        "c_ones_col": f32(np.ones((128, 1))),
        "c_ones_row": f32(np.ones((1, 128))),
        "c_idf": f32(np.eye(128)),
        "c_cst": cst,
    }


def declare_common(nc, p, h):
    shp = {"c_ones_col": [128, 1], "c_ones_row": [1, 128],
           "c_idf": [128, 128], "c_cst": [128, 8], "c_sel": [16, 16 * 128]}
    key = {"c_ones_col": "ones_col", "c_ones_row": "ones_row",
           "c_idf": "idf", "c_cst": "cst", "c_sel": "sel"}
    for name in shp:
        ap = nc.dram_tensor(name, shp[name], F32, kind="ExternalInput").ap()
        sb = h[key[name]]
        p.dma(lambda e, sb=sb, ap=ap: e.dma_start(out=sb[:], in_=ap),
              reads=[], writes=["const"])


def alloc_ln(nc, st, h):
    h["p_stat1"] = st.enter_context(nc.psum_tensor("p_stat1", [1, TT], F32))
    h["p_stat2"] = st.enter_context(nc.psum_tensor("p_stat2", [1, TT], F32))
    h["p_bc"] = st.enter_context(nc.psum_tensor("p_bc", [128, TT], F32))
    for n, shp in [("s_mean", [1, TT]), ("s_msq", [1, TT]), ("s_var", [1, TT]),
                   ("s_sd", [1, TT]), ("s_rstd", [1, TT]), ("s_beta", [1, TT]),
                   ("s_sq", [128, TT]), ("s_abc", [128, TT]), ("s_bbc", [128, TT])]:
        h[n] = st.enter_context(nc.sbuf_tensor(n, shp, F32))


def alloc_gate(nc, st, h, pfx):
    for n, shp in [("s_Lg", [128, 8]), ("s_m1", [128, 1]), ("s_nm1", [128, 1]),
                   ("s_eq", [128, 8]), ("s_L2", [128, 8]), ("s_m2", [128, 1]),
                   ("s_em2", [128, 1]), ("s_den", [128, 1]), ("s_rd", [128, 1]),
                   ("s_geq", [128, 8]), ("s_c", [128, 8]), ("s_e", [128, 8]),
                   ("s_Z", [128, 1])]:
        h[n] = st.enter_context(nc.sbuf_tensor(pfx + n, shp, F32))


def emit_ln(p, h, sb_x, sb_xlf, sb_xlb=None, xk="x"):
    pm, ps, pbc = h["p_stat1"], h["p_stat2"], h["p_bc"]
    s_mean, s_msq, s_var, s_sd, s_rstd, s_beta = (
        h["s_mean"], h["s_msq"], h["s_var"], h["s_sd"], h["s_rstd"], h["s_beta"])
    ones_col, ones_row, cst = h["ones_col"], h["ones_row"], h["cst"]
    sq, abc, bbc = h["s_sq"], h["s_abc"], h["s_bbc"]
    for c in range(DC):
        p.mm(lambda e, c=c: e.matmul(pm[:], ones_col[:], sb_x[:, c, :],
                                     start=(c == 0), stop=(c == DC - 1)),
             reads=[xk, "const", "mean"], writes=["p_s1"])
    p.dve(lambda e: e.tensor_scalar(s_mean[:], pm[:], 1.0 / D, None, ALU.mult),
          reads=["p_s1"], writes=["mean"])
    for c in range(DC):
        p.act(lambda e, c=c: e.activation(sq[:], sb_x[:, c, :], AF.Square),
              reads=[xk], writes=["sq"])
        p.mm(lambda e, c=c: e.matmul(ps[:], ones_col[:], sq[:],
                                     start=(c == 0), stop=(c == DC - 1)),
             reads=["sq", "const"], writes=["p_s2"])
    p.act(lambda e: e.activation(s_msq[:], s_mean[:], AF.Square),
          reads=["mean"], writes=["msq"])
    p.dve(lambda e: e.scalar_tensor_tensor(s_var[:], ps[:], 1.0 / D, s_msq[:],
                                           ALU.mult, ALU.subtract),
          reads=["p_s2", "msq"], writes=["var"])
    p.act(lambda e: e.activation(s_sd[:], s_var[:], AF.Sqrt, bias=cst[:1, 1:2]),
          reads=["var", "const"], writes=["sd"])
    p.dve(lambda e: e.reciprocal(s_rstd[:], s_sd[:]), reads=["sd"], writes=["rstd"])
    p.dve(lambda e: e.scalar_tensor_tensor(s_beta[:], s_mean[:], -1.0, s_rstd[:],
                                           ALU.mult, ALU.mult),
          reads=["mean", "rstd"], writes=["beta"])
    p.mm(lambda e: e.matmul(pbc[:], ones_row[:], s_rstd[:], start=True, stop=True),
         reads=["rstd", "const"], writes=["p_bc"])
    p.act(lambda e: e.activation(abc[:], pbc[:], AF.Copy),
          reads=["p_bc"], writes=["abc"])
    p.mm(lambda e: e.matmul(pbc[:], ones_row[:], s_beta[:], start=True, stop=True),
         reads=["beta", "const"], writes=["p_bc"])
    p.act(lambda e: e.activation(bbc[:], pbc[:], AF.Copy),
          reads=["p_bc"], writes=["bbc"])
    for c in range(DC):
        if sb_xlf is None:
            p.dve(lambda e, c=c: e.tensor_tensor(sq[:], sb_x[:, c, :],
                                                 abc[:], ALU.mult),
                  reads=[xk, "abc"], writes=["sq"])
            p.dve(lambda e, c=c: e.tensor_tensor(sb_xlb[:, c, :], sq[:],
                                                 bbc[:], ALU.add),
                  reads=["sq", "bbc"], writes=["xlb"])
            continue
        p.dve(lambda e, c=c: e.tensor_tensor(sb_xlf[:, c, :], sb_x[:, c, :],
                                             abc[:], ALU.mult),
              reads=[xk, "abc"], writes=["xlf"])
        p.dve(lambda e, c=c: e.tensor_tensor(sb_xlf[:, c, :], sb_xlf[:, c, :],
                                             bbc[:], ALU.add),
              reads=["bbc", "xlf"], writes=["xlf"])
        if sb_xlb is not None:
            p.dve(lambda e, c=c: e.tensor_copy(sb_xlb[:, c, :], sb_xlf[:, c, :]),
                  reads=["xlf"], writes=["xlb"])


def emit_top2(p, h, sb_logT, sb_cmb, use_router_eps):
    idf = h["idf"]
    Lg, m1, nm1, eq, L2, m2, em2, den, rd, geq, ec, ee, Zs = (
        h["s_Lg"], h["s_m1"], h["s_nm1"], h["s_eq"], h["s_L2"], h["s_m2"],
        h["s_em2"], h["s_den"], h["s_rd"], h["s_geq"], h["s_c"], h["s_e"], h["s_Z"])
    ptm = h["p_bc"][:, 0:8]
    pcmb = h["p_bc"][0:8, 0:128]
    K = "p_bc"
    for gq in range(TT // 128):
        p.mm(lambda e, gq=gq: e.transpose(ptm, sb_logT[:, gq * 128:(gq + 1) * 128],
                                          idf[:8, :8]),
             reads=["logT", "const"], writes=[K])
        p.dve(lambda e: e.tensor_copy(Lg[:], ptm), reads=[K], writes=["gLg"])
        p.dve(lambda e: e.tensor_reduce(m1[:], Lg[:], AX, ALU.max),
              reads=["gLg"], writes=["gm1"])
        p.dve(lambda e: e.tensor_scalar(nm1[:], m1[:], -1.0, None, ALU.mult),
              reads=["gm1"], writes=["gnm1"])
        p.act(lambda e: e.activation(ee[:], Lg[:], AF.Exp, bias=nm1[:]),
              reads=["gLg", "gnm1"], writes=["gee"])
        p.dve(lambda e: e.tensor_scalar(eq[:], Lg[:], m1[:], None, ALU.is_equal),
              reads=["gLg", "gm1"], writes=["geq1"])
        p.dve(lambda e: e.scalar_tensor_tensor(L2[:], eq[:], -1e30, Lg[:],
                                               ALU.mult, ALU.add),
              reads=["geq1", "gLg"], writes=["gL2"])
        p.dve(lambda e: e.tensor_reduce(m2[:], L2[:], AX, ALU.max),
              reads=["gL2"], writes=["gm2"])
        p.act(lambda e: e.activation(em2[:], m2[:], AF.Exp, bias=nm1[:]),
              reads=["gm2", "gnm1"], writes=["gem2"])
        if use_router_eps:
            p.dve(lambda e: e.tensor_reduce(Zs[:], ee[:], AX, ALU.add),
                  reads=["gee"], writes=["gZ"])
            p.dve(lambda e: e.scalar_tensor_tensor(den[:], Zs[:], 1e-6, em2[:],
                                                   ALU.mult, ALU.add),
                  reads=["gZ", "gem2"], writes=["gden"])
            p.dve(lambda e: e.tensor_scalar(den[:], den[:], 1.0, None, ALU.add),
                  reads=["gden"], writes=["gden"])
        else:
            p.dve(lambda e: e.tensor_scalar(den[:], em2[:], 1.0, None, ALU.add),
                  reads=["gem2"], writes=["gden"])
        p.dve(lambda e: e.reciprocal(rd[:], den[:]), reads=["gden"], writes=["grd"])
        p.dve(lambda e: e.tensor_scalar(rd[:], rd[:], 0.5, None, ALU.mult),
              reads=["grd"], writes=["grd"])
        p.dve(lambda e: e.tensor_scalar(geq[:], Lg[:], m2[:], None, ALU.is_ge),
              reads=["gLg", "gm2"], writes=["ggeq"])
        p.dve(lambda e: e.tensor_tensor(ec[:], ee[:], geq[:], ALU.mult),
              reads=["gee", "ggeq"], writes=["gec"])
        p.dve(lambda e: e.tensor_scalar(ec[:], ec[:], rd[:], None, ALU.mult),
              reads=["grd", "gec"], writes=["gec"])
        p.mm(lambda e, gq=gq: e.transpose(pcmb, ec[:], idf[:]),
             reads=["gec", "const"], writes=[K])
        p.dve(lambda e, gq=gq: e.tensor_copy(sb_cmb[:, gq * 128:(gq + 1) * 128],
                                             pcmb),
              reads=[K], writes=["cmb"])



def emit_ffn(nc, p, h, st, kind, w, x_ap, y_ap, pfx, cache, cache_mode):
    E = 8
    FF = 4096 if kind == "l0" else 2048
    FFC = FF // 128
    alloc_gate(nc, st, h, pfx)
    sbt = lambda n, shp, dt: st.enter_context(nc.sbuf_tensor(pfx + n, shp, dt))
    sb_x = sbt("x", [128, DC, TT], F32)
    sb_xlf = sbt("xlf", [128, DC, TT], F32)
    sb_xlb = sbt("xlb", [128, DC, TT], BF16)
    sb_mid = sbt("mid", [128, FFC, TT], BF16)
    sb_yacc = sbt("yacc", [128, DC, TT], F32)
    sb_wstg = sbt("wstg", [128, 8, TT], F32)
    sb_w1g = [sbt("w1g0", [128, 8, TT], BF16), sbt("w1g1", [128, 8, TT], BF16)]
    if kind == "l1":
        sb_sg = sbt("sg", [128, TT], F32)
    sb_w2g = [sbt("w2g0", [128, FFC // 2, TT], BF16),
              sbt("w2g1", [128, FFC // 2, TT], BF16)]
    sb_gw = sbt("gw", [128, DC, E], F32)
    sb_cmb = sbt("cmb", [E, TT], F32)
    sb_cb = sbt("cb", [128, TT], F32)
    sb_logT = sbt("logT", [E, TT], F32)
    sb_tmp = [sbt("tmp0", [128, TT], F32), sbt("tmp1", [128, TT], F32)]
    if kind == "l0":
        sb_gb = sbt("gb", [E, 1], F32)
        sb_b1 = sbt("b1", [128, FFC, E], F32)
        sb_b2 = sbt("b2", [E, D], F32)
    p_g = st.enter_context(nc.psum_tensor(pfx + "p_g", [E, TT], F32))
    p_mid = [st.enter_context(nc.psum_tensor(pfx + "p_mid0", [128, TT], F32)),
             st.enter_context(nc.psum_tensor(pfx + "p_mid1", [128, TT], F32))]
    p_out = [st.enter_context(nc.psum_tensor(pfx + "p_out0", [128, TT], F32)),
             st.enter_context(nc.psum_tensor(pfx + "p_out1", [128, TT], F32))]
    w1c, w2c = cache

    p.dma(lambda e: e.dma_start(out=sb_gw[:],
                                in_=w["gw"].rearrange("(c p) e -> p c e", p=128)),
          reads=[], writes=["gw"])
    if kind == "l0":
        p.dma(lambda e: e.dma_start(out=sb_gb[:], in_=w["gb"]),
              reads=[], writes=["gb"])
        p.dma(lambda e: e.dma_start(out=sb_b1[:],
                                    in_=w["b1t"].rearrange("(c p) e -> p c e", p=128)),
              reads=[], writes=["b1"])
        p.dma(lambda e: e.dma_start(out=sb_b2[:], in_=w["b2"]),
              reads=[], writes=["b2"])
    w1_d, w2_d = w["w1"], w["w2"]

    def w1_load(t, ex, cg, buf):
        kb = f"w1g{buf}"
        kc = f"wc1:{ex}:{cg}"
        if cache_mode == "fill" and t == 0:
            p.dma(lambda e, ex=ex, cg=cg: e.dma_start(
                out=sb_wstg[:],
                in_=w1_d[ex, :, cg * TT:(cg + 1) * TT]
                    .rearrange("(c p) m -> p c m", p=128)),
                reads=[], writes=["wstg"])
            p.dve(lambda e, buf=buf: e.tensor_copy(sb_w1g[buf][:], sb_wstg[:]),
                  reads=["wstg"], writes=[kb])
            p.dma2(lambda e, ex=ex, cg=cg, buf=buf: e.dma_start(
                out=w1c[ex, :, cg * TT:(cg + 1) * TT]
                    .rearrange("(c p) m -> p c m", p=128),
                in_=sb_w1g[buf][:]),
                reads=[kb], writes=[kc])
        else:
            p.dma(lambda e, ex=ex, cg=cg, buf=buf: e.dma_start(
                out=sb_w1g[buf][:],
                in_=w1c[ex, :, cg * TT:(cg + 1) * TT]
                    .rearrange("(c p) m -> p c m", p=128)),
                reads=[kc], writes=[kb])

    for t in range(NTILE):
        t0 = t * TT
        p.dma(lambda e, t0=t0: e.dma_start(
            out=sb_x[:], in_=x_ap[:, t0:t0 + TT].rearrange("(c p) m -> p c m", p=128)),
            reads=[f"X:{t}"], writes=["x"])
        emit_ln(p, h, sb_x, sb_xlf, sb_xlb)
        for c in range(DC):
            p.mm(lambda e, c=c: e.matmul(p_g[:], sb_gw[:, c, :], sb_xlf[:, c, :],
                                         start=(c == 0), stop=(c == DC - 1)),
                 reads=["gw", "xlf", "logT"], writes=["p_g"])
        if kind == "l0":
            p.dve(lambda e: e.tensor_scalar(sb_logT[:], p_g[:], sb_gb[:],
                                            None, ALU.add),
                  reads=["p_g", "gb"], writes=["logT"])
        else:
            p.act(lambda e: e.activation(sb_logT[:], p_g[:], AF.Copy),
                  reads=["p_g"], writes=["logT"])
        emit_top2(p, h, sb_logT, sb_cmb, use_router_eps=(kind == "l1"))
        if kind == "l0":
            for fc in range(DC):
                b = fc % 2
                p.mm(lambda e, fc=fc, b=b: e.matmul(
                    p_out[b][:], sb_b2[:, fc * 128:(fc + 1) * 128], sb_cmb[:],
                    start=True, stop=True),
                    reads=["b2", "cmb"], writes=[f"po{b}"])
                p.act(lambda e, fc=fc, b=b: e.activation(sb_yacc[:, fc, :],
                                                         p_out[b][:], AF.Copy),
                      reads=[f"po{b}"], writes=[f"yacc{fc}"])
        else:
            p.dve(lambda e: e.memset(sb_yacc[:], 0.0),
                  reads=[], writes=[f"yacc{fc}" for fc in range(DC)])

        for ex in range(E):
            p.mm(lambda e, ex=ex: e.matmul(
                h["p_bc"][:], h["sel"][0:8, ex * 128:(ex + 1) * 128],
                sb_cmb[:], start=True, stop=True),
                reads=["sel", "cmb", "const"], writes=["p_bc"])
            p.act(lambda e: e.activation(sb_cb[:], h["p_bc"][:], AF.Copy),
                  reads=["p_bc"], writes=["cb"])
            if kind == "l0":
                for cg in range(FF // TT):
                    buf = cg % 2
                    w1_load(t, ex, cg, buf)
                    for fl in range(TT // 128):
                        fc = cg * (TT // 128) + fl
                        mb = fc % 2
                        for k in range(DC):
                            p.mm(lambda e, k=k, fl=fl, buf=buf, mb=mb: e.matmul(
                                p_mid[mb][:], sb_w1g[buf][:, k, fl * 128:(fl + 1) * 128],
                                sb_xlb[:, k, :], start=(k == 0), stop=(k == DC - 1)),
                                reads=[f"w1g{buf}", "xlb"], writes=[f"pm{mb}"])
                        p.act(lambda e, fc=fc, ex=ex, mb=mb: e.activation(
                            sb_mid[:, fc, :], p_mid[mb][:], AF.Relu,
                            bias=sb_b1[:, fc, ex:ex + 1]),
                            reads=[f"pm{mb}", "b1"], writes=[f"mid{fc}"])
            else:
                for cg in range(FF // TT):
                    w1_load(t, ex, cg, 0)
                    w1_load(t, ex, (FF // TT) + cg, 1)
                    for fl in range(TT // 128):
                        fc = cg * (TT // 128) + fl
                        for k in range(DC):
                            p.mm(lambda e, k=k, fl=fl: e.matmul(
                                p_mid[0][:], sb_w1g[0][:, k, fl * 128:(fl + 1) * 128],
                                sb_xlb[:, k, :], start=(k == 0), stop=(k == DC - 1)),
                                reads=["w1g0", "xlb"], writes=["pm0"])
                        p.act(lambda e: e.activation(sb_sg[:], p_mid[0][:], AF.Silu),
                              reads=["pm0"], writes=["sg"])
                        for k in range(DC):
                            p.mm(lambda e, k=k, fl=fl: e.matmul(
                                p_mid[1][:], sb_w1g[1][:, k, fl * 128:(fl + 1) * 128],
                                sb_xlb[:, k, :], start=(k == 0), stop=(k == DC - 1)),
                                reads=["w1g1", "xlb"], writes=["pm1"])
                        p.dve(lambda e, fc=fc: e.tensor_tensor(
                            sb_mid[:, fc, :], p_mid[1][:], sb_sg[:], ALU.mult),
                            reads=["pm1", "sg"], writes=[f"mid{fc}"])
            for cg2 in range(D // TT):
                for hf in range(2):
                    kb = f"w2g{hf}"
                    kc = f"wc2:{ex}:{cg2}:{hf}"
                    r0 = hf * (FF // 2)
                    if cache_mode == "fill" and t == 0:
                        for q in range(FFC // 16):
                            p.dma(lambda e, ex=ex, cg2=cg2, r0=r0, q=q: e.dma_start(
                                out=sb_wstg[:],
                                in_=w2_d[ex, r0 + q * 1024:r0 + (q + 1) * 1024,
                                         cg2 * TT:(cg2 + 1) * TT]
                                    .rearrange("(c p) m -> p c m", p=128)),
                                reads=[], writes=["wstg"])
                            p.dve(lambda e, hf=hf, q=q: e.tensor_copy(
                                sb_w2g[hf][:, q * 8:(q + 1) * 8, :], sb_wstg[:]),
                                reads=["wstg"], writes=[kb])
                        p.dma2(lambda e, ex=ex, cg2=cg2, r0=r0, hf=hf: e.dma_start(
                            out=w2c[ex, r0:r0 + FF // 2, cg2 * TT:(cg2 + 1) * TT]
                                .rearrange("(c p) m -> p c m", p=128),
                            in_=sb_w2g[hf][:]),
                            reads=[kb], writes=[kc])
                    else:
                        p.dma(lambda e, ex=ex, cg2=cg2, r0=r0, hf=hf: e.dma_start(
                            out=sb_w2g[hf][:],
                            in_=w2c[ex, r0:r0 + FF // 2, cg2 * TT:(cg2 + 1) * TT]
                                .rearrange("(c p) m -> p c m", p=128)),
                            reads=[kc], writes=[kb])
                for fl in range(TT // 128):
                    fc = cg2 * (TT // 128) + fl
                    b = fc % 2
                    for k in range(FFC):
                        hf = k // (FFC // 2)
                        kk = k % (FFC // 2)
                        p.mm(lambda e, kk=kk, hf=hf, fl=fl, b=b, k=k: e.matmul(
                            p_out[b][:], sb_w2g[hf][:, kk, fl * 128:(fl + 1) * 128],
                            sb_mid[:, k, :],
                            start=(k == 0), stop=(k == FFC - 1)),
                            reads=[f"w2g{hf}", f"mid{k}"], writes=[f"po{b}"])
                    p.dve(lambda e, b=b: e.tensor_tensor(
                        sb_tmp[b][:], p_out[b][:], sb_cb[:], ALU.mult),
                        reads=[f"po{b}", "cb"], writes=[f"tmp{b}"])
                    p.pool(lambda e, fc=fc, b=b: e.tensor_tensor(
                        sb_yacc[:, fc, :], sb_yacc[:, fc, :], sb_tmp[b][:], ALU.add),
                        reads=[f"tmp{b}", f"yacc{fc}"], writes=[f"yacc{fc}"])
        for c in range(DC):
            p.dve(lambda e, c=c: e.tensor_tensor(sb_x[:, c, :], sb_x[:, c, :],
                                                 sb_yacc[:, c, :], ALU.add),
                  reads=[f"yacc{c}", "x"], writes=["x"])
        p.dma2(lambda e, t0=t0: e.dma_start(
            out=y_ap[:, t0:t0 + TT].rearrange("(c p) m -> p c m", p=128),
            in_=sb_x[:]),
            reads=["x"], writes=[f"Y:{t}"])


def declare_ffn_weights(nc, kind, pfx):
    E, FF = 8, (4096 if kind == "l0" else 2048)
    w = {}
    w["gw"] = nc.dram_tensor(pfx + "gw", [D, E], F32, kind="ExternalInput").ap()
    if kind == "l0":
        w["gb"] = nc.dram_tensor(pfx + "gb", [E, 1], F32, kind="ExternalInput").ap()
        w["w1"] = nc.dram_tensor(pfx + "w1", [E, D, FF], F32, kind="ExternalInput").ap()
        w["b1t"] = nc.dram_tensor(pfx + "b1t", [FF, E], F32, kind="ExternalInput").ap()
        w["w2"] = nc.dram_tensor(pfx + "w2", [E, FF, D], F32, kind="ExternalInput").ap()
        w["b2"] = nc.dram_tensor(pfx + "b2", [E, D], F32, kind="ExternalInput").ap()
    else:
        w["w1"] = nc.dram_tensor(pfx + "w13", [E, D, 2 * FF], F32,
                                 kind="ExternalInput").ap()
        w["w2"] = nc.dram_tensor(pfx + "w2", [E, FF, D], F32, kind="ExternalInput").ap()
    return w


def ffn_weight_map(kind, pfx, ins):
    m = {}
    if kind == "l0":
        m[pfx + "gw"] = f32(ins["L0_ffn_gate_w"])
        m[pfx + "gb"] = f32(np.asarray(ins["L0_ffn_gate_b"]).reshape(8, 1))
        m[pfx + "w1"] = f32(ins["L0_ffn_w1"])
        m[pfx + "b1t"] = f32(np.asarray(ins["L0_ffn_b1"]).T)
        m[pfx + "w2"] = f32(ins["L0_ffn_w2"])
        m[pfx + "b2"] = f32(ins["L0_ffn_b2"])
    else:
        m[pfx + "gw"] = f32(np.asarray(ins["L1_ffn_router_w"]).T)
        m[pfx + "w13"] = f32(ins["L1_ffn_w13"])
        m[pfx + "w2"] = f32(ins["L1_ffn_w2"])
    return m



ATTN_PARTS = "ABC"


def emit_l0_attn(nc, p, h, st, w, x_ap, y_ap):
    NH = 16
    q_i = nc.dram_tensor("at_q_i", [D, T], BF16, kind="Internal").ap()
    k_i = nc.dram_tensor("at_k_i", [D, T], BF16, kind="Internal").ap()
    v_i = nc.dram_tensor("at_v_i", [T, NH * 65], BF16, kind="Internal").ap()
    xr_i = nc.dram_tensor("at_xr_i", [D, T], F32, kind="Internal").ap()
    a_i = nc.dram_tensor("at_a_i", [D, T], BF16, kind="Internal").ap()

    sbt = lambda n, shp, dt: st.enter_context(nc.sbuf_tensor("at" + n, shp, dt))
    sb_x = sbt("x", [128, DC, TT], F32)
    sb_xlf = sbt("xlf", [128, DC, TT], F32)
    sb_xlb = sbt("xlb", [128, DC, TT], BF16)
    sb_xr = [sbt("xr0", [128, TT], F32), sbt("xr1", [128, TT], F32)]
    sb_wstg = sbt("wstg", [128, 8, TT], F32)
    sb_wb = sbt("wb", [128, DC, D], BF16)
    sb_proj = sbt("proj", [128, DC, TT], BF16)
    sb_v65 = sbt("v65", [128, NH * 65], BF16)
    sb_kh = sbt("kh", [64, T], BF16)
    sb_vh = sbt("vh", [128, T // 128, 65], BF16)
    sb_qh = sbt("qh", [64, TT], BF16)
    sb_S = sbt("S", [128, T // 128, TT], BF16)
    sb_m = sbt("m", [128, TT], F32)
    sb_cmx = sbt("cmx", [128, 1], F32)
    sb_cmr = sbt("cmr", [1, TT], F32)
    sb_nmb = sbt("nmb", [128, TT], F32)
    sb_esc = [sbt("esc0", [128, TT], F32), sbt("esc1", [128, TT], F32)]
    sb_inv = sbt("inv", [1, TT], F32)
    sb_invb = sbt("invb", [128, TT], F32)
    sb_an = sbt("an", [64, TT], BF16)
    sb_Ag = sbt("Ag", [128, DC, TT], BF16)
    sb_out = [sbt("out0", [128, TT], F32), sbt("out1", [128, TT], F32)]
    p_s = [st.enter_context(nc.psum_tensor("at_p_s0", [128, TT], F32)),
           st.enter_context(nc.psum_tensor("at_p_s1", [128, TT], F32))]
    _pmm = st.enter_context(nc.psum_tensor("at_p_mm0", [128, TT], F32))
    p_mm = [_pmm, _pmm]
    p_tmT = st.enter_context(nc.psum_tensor("at_p_tm", [128, 128], F32))
    p_oT = st.enter_context(nc.psum_tensor("at_p_o", [65, TT], F32))
    p_av = p_oT
    p_o = p_oT[:]
    p_tm = p_tmT[:]
    p_tr = p_tmT[0:1, 0:128]

    def load_weight(w_d, wk):
        for half in range(2):
            p.dma(lambda e, half=half: e.dma_start(
                out=sb_wstg[:],
                in_=w_d[:, half * TT:(half + 1) * TT]
                    .rearrange("(c p) m -> p c m", p=128)),
                reads=[], writes=["wstg"])
            p.dve(lambda e, half=half: e.tensor_copy(
                sb_wb[:, :, half * TT:(half + 1) * TT], sb_wstg[:]),
                reads=["wstg"], writes=["wb"])

    for t in (range(NTILE) if "A" in ATTN_PARTS else []):
        t0 = t * TT
        p.dma(lambda e, t0=t0: e.dma_start(
            out=sb_x[:], in_=x_ap[:, t0:t0 + TT].rearrange("(c p) m -> p c m", p=128)),
            reads=[], writes=["x"])
        emit_ln(p, h, sb_x, sb_xlf, sb_xlb)
        for c in range(DC):
            rb = c % 2
            p.dve(lambda e, c=c, rb=rb: e.tensor_tensor(sb_xr[rb][:], sb_x[:, c, :],
                                                        sb_xlf[:, c, :], ALU.add),
                  reads=["x", "xlf"], writes=[f"xr{rb}"])
            p.dma2(lambda e, c=c, t0=t0, rb=rb: e.dma_start(
                out=xr_i[c * 128:(c + 1) * 128, t0:t0 + TT], in_=sb_xr[rb][:]),
                reads=[f"xr{rb}"], writes=["xr_i"])
        for w_d, dst, scale, wk in ((w["wq"], q_i, 0.125, "q_i"),
                                    (w["wk"], k_i, 1.0, "k_i")):
            load_weight(w_d, wk)
            for mc in range(DC):
                b = mc % 2
                for k in range(DC):
                    p.mm(lambda e, mc=mc, k=k, b=b: e.matmul(
                        p_mm[b][:], sb_wb[:, k, mc * 128:(mc + 1) * 128],
                        sb_xlb[:, k, :], start=(k == 0), stop=(k == DC - 1)),
                        reads=["wb", "xlb"], writes=["pmm0"])
                p.act(lambda e, mc=mc, scale=scale, b=b: e.activation(
                    sb_proj[:, mc, :], p_mm[b][:], AF.Copy, scale=scale),
                    reads=["pmm0"], writes=["proj"])
            p.dma2(lambda e, dst=dst, t0=t0: e.dma_start(
                out=dst[:, t0:t0 + TT].rearrange("(c p) m -> p c m", p=128),
                in_=sb_proj[:]),
                reads=["proj"], writes=[wk])
        load_weight(w["wv"], "v")
        for tc in range(TT // 128):
            for half in range(2):
                b = half
                for k in range(DC):
                    p.mm(lambda e, tc=tc, half=half, k=k, b=b: e.matmul(
                        p_mm[b][:], sb_xlb[:, k, tc * 128:(tc + 1) * 128],
                        sb_wb[:, k, half * TT:(half + 1) * TT],
                        start=(k == 0), stop=(k == DC - 1)),
                        reads=["wb", "xlb"], writes=["pmm0"])
                p.dve(lambda e, half=half, b=b: e.tensor_copy(
                    sb_v65[:, half * 520:(half + 1) * 520]
                        .rearrange("p (hh q) -> p hh q", q=65)[:, :, 0:64],
                    p_mm[b][:].rearrange("p (hh q) -> p hh q", q=64)),
                    reads=["pmm0"], writes=["v65"])
            p.dve(lambda e: e.memset(
                sb_v65[:].rearrange("p (hh q) -> p hh q", q=65)[:, :, 64:65], 1.0),
                reads=[], writes=["v65"])
            p.dma2(lambda e, t0=t0, tc=tc: e.dma_start(
                out=v_i[t0 + tc * 128:t0 + (tc + 1) * 128, :], in_=sb_v65[:]),
                reads=["v65"], writes=["v_i"])

    NTC = T // 128
    import os as _os
    _nh = int(_os.environ.get("ATTN_NH", NH))
    _ng = int(_os.environ.get("ATTN_NG", NTILE))
    for hh in (range(_nh) if "B" in ATTN_PARTS else []):
        p.dma(lambda e, hh=hh: e.dma_start(
            out=sb_kh[:], in_=k_i[hh * 64:(hh + 1) * 64, :]),
            reads=["k_i"], writes=["kh"])
        p.dma(lambda e, hh=hh: e.dma_start(
            out=sb_vh[:], in_=v_i[:, hh * 65:(hh + 1) * 65]
                .rearrange("(c p) n -> p c n", p=128)),
            reads=["v_i"], writes=["vh"])
        for grp in range(_ng):
            g0 = grp * TT
            p.dma(lambda e, hh=hh, g0=g0: e.dma_start(
                out=sb_qh[:], in_=q_i[hh * 64:(hh + 1) * 64, g0:g0 + TT]),
                reads=["q_i"], writes=["qh"])
            for tc in range(NTC):
                b = tc % 2
                p.mm(lambda e, tc=tc, b=b: e.matmul(
                    p_s[b][:], sb_kh[:, tc * 128:(tc + 1) * 128], sb_qh[:],
                    start=True, stop=True),
                    reads=["kh", "qh"], writes=[f"ps{b}"])
                p.act(lambda e, tc=tc, b=b: e.activation(sb_S[:, tc, :], p_s[b][:],
                                                         AF.Copy),
                      reads=[f"ps{b}"], writes=[f"S{tc}"])
                if tc == 0:
                    p.dve(lambda e, tc=tc: e.tensor_copy(sb_m[:], sb_S[:, tc, :]),
                          reads=[f"S{tc}"], writes=["m"])
                else:
                    p.dve(lambda e, tc=tc: e.tensor_tensor(sb_m[:], sb_m[:],
                                                           sb_S[:, tc, :], ALU.max),
                          reads=[f"S{tc}", "m"], writes=["m"])
            _stage = int(_os.environ.get("ATTN_STAGE", "9"))
            for bb in (range(TT // 128) if _stage >= 1 else []):
                p.mm(lambda e, bb=bb: e.transpose(
                    p_tm, sb_m[:, bb * 128:(bb + 1) * 128], h["idf"][:]),
                    reads=["m", "const"], writes=["ptm"])
                p.dve(lambda e: e.tensor_reduce(sb_cmx[:], p_tm, AX, ALU.max),
                      reads=["ptm"], writes=["cmx"])
                p.mm(lambda e: e.transpose(p_tr, sb_cmx[:], h["idf"][:]),
                     reads=["cmx", "const"], writes=["ptm"])
                p.dve(lambda e, bb=bb: e.tensor_copy(
                    sb_cmr[:, bb * 128:(bb + 1) * 128], p_tr),
                    reads=["ptm"], writes=["cmr"])
            if _stage >= 2:
                p.mm(lambda e: e.matmul(h["p_bc"][:], h["ones_row"][:], sb_cmr[:],
                                        start=True, stop=True),
                     reads=["cmr", "const"], writes=["p_bc"])
                p.act(lambda e: e.activation(sb_nmb[:], h["p_bc"][:], AF.Copy,
                                             scale=-1.0),
                      reads=["p_bc"], writes=["nmb"])
            for tc in (range(NTC) if _stage >= 3 else []):
                eb = tc % 2
                p.dve(lambda e, tc=tc, eb=eb: e.tensor_tensor(
                    sb_esc[eb][:], sb_S[:, tc, :], sb_nmb[:], ALU.add),
                    reads=[f"S{tc}", "nmb"], writes=[f"esc{eb}"])
                p.act(lambda e, tc=tc, eb=eb: e.activation(sb_S[:, tc, :],
                                                           sb_esc[eb][:], AF.Exp),
                      reads=[f"esc{eb}"], writes=[f"S{tc}"])
            for tc in (range(NTC) if _stage >= 4 else []):
                p.mm(lambda e, tc=tc: e.matmul(
                    p_o, sb_vh[:, tc, :], sb_S[:, tc, :],
                    start=(tc == 0), stop=(tc == NTC - 1)),
                    reads=["vh", f"S{tc}"], writes=["po"])
            if _stage < 5:
                continue
            p.dve(lambda e: e.reciprocal(sb_inv[:], p_oT[64:65, :]),
                  reads=["po"], writes=["inv"])
            p.mm(lambda e: e.matmul(h["p_bc"][:], h["ones_row"][:], sb_inv[:],
                                    start=True, stop=True),
                 reads=["inv", "const"], writes=["p_bc"])
            p.act(lambda e: e.activation(sb_invb[:], h["p_bc"][:], AF.Copy),
                  reads=["p_bc"], writes=["invb"])
            p.dve(lambda e: e.tensor_tensor(
                sb_an[:], p_oT[0:64, :], sb_invb[0:64, :], ALU.mult),
                reads=["po", "invb"], writes=["an"])
            p.dma2(lambda e, hh=hh, g0=g0: e.dma_start(
                out=a_i[hh * 64:(hh + 1) * 64, g0:g0 + TT], in_=sb_an[:]),
                reads=["an"], writes=["a_i"])

    load_weight(w["wp"], "wp")
    for grp in (range(NTILE) if "C" in ATTN_PARTS else []):
        g0 = grp * TT
        p.dma(lambda e, g0=g0: e.dma_start(
            out=sb_proj[:], in_=a_i[:, g0:g0 + TT].rearrange("(c p) m -> p c m", p=128)),
            reads=["a_i"], writes=["proj"])
        for c in range(DC):
            p.act(lambda e, c=c: e.activation(
                sb_Ag[:, c, :], sb_proj[:, c, :], AF.Gelu),
                reads=["proj"], writes=["Ag"])
        for mc in range(DC):
            b = mc % 2
            for k in range(DC):
                p.mm(lambda e, mc=mc, k=k, b=b: e.matmul(
                    p_mm[b][:], sb_wb[:, k, mc * 128:(mc + 1) * 128],
                    sb_Ag[:, k, :], start=(k == 0), stop=(k == DC - 1)),
                    reads=["wb", "Ag"], writes=["pmm0"])
            p.dma(lambda e, mc=mc, g0=g0, b=b: e.dma_start(
                out=sb_xr[b][:], in_=xr_i[mc * 128:(mc + 1) * 128, g0:g0 + TT]),
                reads=["xr_i"], writes=[f"xr{b}"])
            p.dve(lambda e, b=b: e.tensor_tensor(sb_out[b][:], p_mm[b][:],
                                                 sb_xr[b][:], ALU.add),
                  reads=["pmm0", f"xr{b}"], writes=[f"out{b}"])
            p.dma2(lambda e, mc=mc, g0=g0, b=b: e.dma_start(
                out=y_ap[mc * 128:(mc + 1) * 128, g0:g0 + TT], in_=sb_out[b][:]),
                reads=[f"out{b}"], writes=["Y"])


def declare_l0_attn_weights(nc):
    w = {}
    for n in ("wq", "wk", "wv", "wp"):
        w[n] = nc.dram_tensor("at_" + n, [D, D], F32, kind="ExternalInput").ap()
    return w


def l0_attn_weight_map(ins):
    return {"at_wq": f32(ins["L0_attn_Wq"]), "at_wk": f32(ins["L0_attn_Wk"]),
            "at_wv": f32(ins["L0_attn_Wv"]), "at_wp": f32(ins["L0_attn_Wpost"])}



def emit_l1_attn(nc, p, h, st, w, x_ap, y_ap):
    DI, DS = 2048, 16
    DIC = DI // 128
    TC = 512
    NCH = T // TC
    u_i = nc.dram_tensor("ma_u_i", [DI, T], BF16, kind="Internal").ap()
    sr_i = nc.dram_tensor("ma_sr_i", [DI, T], BF16, kind="Internal").ap()
    dl_i = nc.dram_tensor("ma_dl_i", [DI, T], F32, kind="Internal").ap()
    du_i = nc.dram_tensor("ma_du_i", [DI, T], BF16, kind="Internal").ap()
    bt_i = nc.dram_tensor("ma_bt_i", [DS, T], F32, kind="Internal").ap()
    ct_i = nc.dram_tensor("ma_ct_i", [DS, T], F32, kind="Internal").ap()

    sbt = lambda n, shp, dt: st.enter_context(nc.sbuf_tensor("ma" + n, shp, dt))
    sb_x = sbt("x", [128, DC, TT], F32)
    sb_xlb = sbt("xlb", [128, DC, TT], BF16)
    sb_u = sbt("u", [128, DIC, TT], BF16)
    _wstg = sbt("wstg0", [128, 8, TT], F32)
    sb_wstg = [_wstg, _wstg]
    sb_wd = sbt("wd", [128, DIC, TT], BF16)
    sb_wbc = sbt("wbc", [128, DIC, 2 * DS], BF16)
    sb_wbcf = _wstg[:, 0, 0:DIC * 2 * DS].rearrange("p (c n) -> p c n", n=2 * DS)
    _ch1 = sbt("ch1a", [128, TT], F32)
    sb_ch1 = [_ch1, _ch1]
    sb_ch2 = sbt("ch2", [128, TT], F32)
    _chb = sbt("chba", [128, TT], BF16)
    sb_chb = [_chb, _chb]
    sb_bc = sbt("bc", [DS, TT], F32)
    sb_A = sbt("A", [128, DIC * DS], F32)
    sb_bd = sbt("bd", [128, DIC], F32)
    sb_dp = sbt("dp", [128, DIC], F32)
    sb_dl = sbt("dl", [128, DIC, TC], F32)
    sb_btc = sbt("btc", [DS, TC], F32)
    sb_ctc = sbt("ctc", [DS, TC], F32)
    sb_brep = sbt("brep", [128, TC], BF16)
    sb_crep = sbt("crep", [128, TC], BF16)
    sb_a = [sbt("a0", [128, TC], BF16), sbt("a1", [128, TC], BF16)]
    sb_b = [sbt("b0", [128, TC], BF16), sbt("b1", [128, TC], BF16)]
    sb_htr = [sbt("htr0", [128, TC], F32), sbt("htr1", [128, TC], F32)]
    _yc = sbt("yc0", [128, TC], F32)
    sb_yc = [_yc, _yc]
    sb_hst = sbt("hst", [128, DIC * DS], F32)
    sb_y = sbt("y", [128, DIC, TC], F32)
    sb_sr = sbt("sr", [128, DIC, TC], BF16)
    _o = sbt("o0", [128, TT], F32)
    sb_o = [_o, _o]
    p_mm = [st.enter_context(nc.psum_tensor("ma_p_mm0", [128, TT], F32)),
            st.enter_context(nc.psum_tensor("ma_p_mm1", [128, TT], F32))]
    p_bc2 = st.enter_context(nc.psum_tensor("ma_p_bc2", [128, TC], F32))
    p_sm = st.enter_context(nc.psum_tensor("ma_p_sm", [DS, TT], F32))

    p.dma(lambda e: e.dma_start(out=sb_bd[:], in_=w["bdl"]), reads=[], writes=["bd"])
    p.dma(lambda e: e.dma_start(out=sb_dp[:], in_=w["dpar"]), reads=[], writes=["dp"])
    p.dma(lambda e: e.dma_start(
        out=sb_A[:].rearrange("p (c n) -> p c n", n=DS),
        in_=w["alog"].rearrange("(c p) n -> p c n", p=128)),
        reads=[], writes=["A"])
    p.act(lambda e: e.activation(sb_A[:], sb_A[:], AF.Exp), reads=["A"], writes=["A"])
    p.dve(lambda e: e.tensor_scalar(sb_A[:], sb_A[:], -1.0, None, ALU.mult),
          reads=["A"], writes=["A"])
    p.dma(lambda e: e.dma_start(
        out=sb_wbcf[:, :, 0:DS], in_=w["wB"].rearrange("(c p) n -> p c n", p=128)),
        reads=[], writes=["wstg0"])
    p.dma(lambda e: e.dma_start(
        out=sb_wbcf[:, :, DS:2 * DS], in_=w["wC"].rearrange("(c p) n -> p c n", p=128)),
        reads=[], writes=["wstg0"])
    p.dve(lambda e: e.tensor_copy(sb_wbc[:], sb_wbcf),
          reads=["wstg0"], writes=["wbc"])

    def stream_w(w_ap, rows, cols, dst, dst_k0, sbuf):
        ks = "wstg0"
        p.dma(lambda e: e.dma_start(
            out=sb_wstg[sbuf][:], in_=w_ap[rows[0]:rows[1], cols[0]:cols[1]]
                .rearrange("(c p) m -> p c m", p=128)),
            reads=[], writes=[ks])
        p.dve(lambda e: e.tensor_copy(dst[:, dst_k0:dst_k0 + 8, :], sb_wstg[sbuf][:]),
              reads=[ks], writes=["wd"])

    import os as _os
    _parts = _os.environ.get("MAMBA_PARTS", "A1234SO")
    for t in (range(NTILE) if "A" in _parts else []):
        t0 = t * TT
        p.dma(lambda e, t0=t0: e.dma_start(
            out=sb_x[:], in_=x_ap[:, t0:t0 + TT].rearrange("(c p) m -> p c m", p=128)),
            reads=[], writes=["x"])
        emit_ln(p, h, sb_x, None, sb_xlb)
        _pms = bool(_os.environ.get("MAMBA_PMM_SINGLE"))
        for cg in (range(DI // TT) if "1" in _parts else []):
            stream_w(w["win"], (0, D), (cg * TT, (cg + 1) * TT), sb_wd, 0, cg % 2)
            for fl in range(TT // 128):
                mc = cg * (TT // 128) + fl
                b = 0 if _pms else mc % 2
                for k in range(DC):
                    p.mm(lambda e, k=k, fl=fl, b=b: e.matmul(
                        p_mm[b][:], sb_wd[:, k, fl * 128:(fl + 1) * 128],
                        sb_xlb[:, k, :], start=(k == 0), stop=(k == DC - 1)),
                        reads=["wd", "xlb"], writes=[f"pmm{b}"])
                if "E" not in _os.environ.get("MAMBA_SKIP", ""):
                    p.act(lambda e, mc=mc, b=b: e.activation(
                        sb_u[:, mc, :], p_mm[b][:], AF.Silu),
                        reads=[f"pmm{b}"], writes=["u"])
        if "U" not in _os.environ.get("MAMBA_SKIP", ""):
            p.dma2(lambda e, t0=t0: e.dma_start(
                out=u_i[:, t0:t0 + TT].rearrange("(c p) m -> p c m", p=128),
                in_=sb_u[:]),
                reads=["u"], writes=["u_i"])
        for cg in (range(DI // TT) if "2" in _parts else []):
            stream_w(w["win"], (0, D), (DI + cg * TT, DI + (cg + 1) * TT), sb_wd, 0,
                     cg % 2)
            for fl in range(TT // 128):
                mc = cg * (TT // 128) + fl
                b = mc % 2
                for k in range(DC):
                    p.mm(lambda e, k=k, fl=fl, b=b: e.matmul(
                        p_mm[b][:], sb_wd[:, k, fl * 128:(fl + 1) * 128],
                        sb_xlb[:, k, :], start=(k == 0), stop=(k == DC - 1)),
                        reads=["wd", "xlb"], writes=[f"pmm{b}"])
                p.act(lambda e, b=b: e.activation(sb_chb[b][:], p_mm[b][:], AF.Silu),
                      reads=[f"pmm{b}"], writes=["chb0"])
                p.dma2(lambda e, mc=mc, t0=t0, b=b: e.dma_start(
                    out=sr_i[mc * 128:(mc + 1) * 128, t0:t0 + TT], in_=sb_chb[b][:]),
                    reads=["chb0"], writes=["sr_i"])
        for cg in (range(DI // TT) if "3" in _parts else []):
            stream_w(w["wdl"], (0, 1024), (cg * TT, (cg + 1) * TT), sb_wd, 0, 0)
            stream_w(w["wdl"], (1024, 2048), (cg * TT, (cg + 1) * TT), sb_wd, 8, 1)
            for fl in range(TT // 128):
                mc = cg * (TT // 128) + fl
                b = mc % 2
                for k in range(DIC):
                    p.mm(lambda e, k=k, fl=fl, b=b: e.matmul(
                        p_mm[b][:], sb_wd[:, k, fl * 128:(fl + 1) * 128],
                        sb_u[:, k, :], start=(k == 0), stop=(k == DIC - 1)),
                        reads=["wd", "u"], writes=[f"pmm{b}"])
                p.act(lambda e, mc=mc, b=b: e.activation(
                    sb_ch1[b][:], p_mm[b][:], AF.Relu, bias=sb_bd[:, mc:mc + 1]),
                    reads=[f"pmm{b}", "bd"], writes=["ch1x"])
                p.act(lambda e, mc=mc, b=b: e.activation(
                    sb_ch2[:], p_mm[b][:], AF.Abs, bias=sb_bd[:, mc:mc + 1]),
                    reads=[f"pmm{b}", "bd"], writes=["ch2"])
                p.act(lambda e: e.activation(sb_ch2[:], sb_ch2[:], AF.Exp, scale=-1.0),
                      reads=["ch2"], writes=["ch2"])
                p.act(lambda e: e.activation(sb_ch2[:], sb_ch2[:], AF.Ln,
                                             bias=h["cst"][:, 0:1]),
                      reads=["ch2", "const"], writes=["ch2"])
                p.dve(lambda e, b=b: e.tensor_tensor(sb_ch1[b][:], sb_ch1[b][:],
                                                     sb_ch2[:], ALU.add),
                      reads=["ch2", "ch1x"], writes=["ch1x"])
                p.dma2(lambda e, mc=mc, t0=t0, b=b: e.dma_start(
                    out=dl_i[mc * 128:(mc + 1) * 128, t0:t0 + TT], in_=sb_ch1[b][:]),
                    reads=["ch1x"], writes=["dl_i"])
                p.dve(lambda e, mc=mc, b=b: e.tensor_tensor(
                    sb_chb[b][:], sb_ch1[b][:], sb_u[:, mc, :], ALU.mult),
                    reads=["ch1x", "u"], writes=["chb0"])
                p.dma2(lambda e, mc=mc, t0=t0, b=b: e.dma_start(
                    out=du_i[mc * 128:(mc + 1) * 128, t0:t0 + TT], in_=sb_chb[b][:]),
                    reads=["chb0"], writes=["du_i"])
        for which, dst, dk in (((0, bt_i, "bt_i"), (1, ct_i, "ct_i"))
                               if "4" in _parts else ()):
            for k in range(DIC):
                p.mm(lambda e, k=k, which=which: e.matmul(
                    p_sm[:], sb_wbc[:, k, which * DS:(which + 1) * DS],
                    sb_u[:, k, :], start=(k == 0), stop=(k == DIC - 1)),
                    reads=["wbc", "u"], writes=["psm"])
            p.act(lambda e: e.activation(sb_bc[:], p_sm[:], AF.Copy),
                  reads=["psm"], writes=["bc"])
            p.dma2(lambda e, dst=dst, t0=t0: e.dma_start(
                out=dst[:, t0:t0 + TT], in_=sb_bc[:]),
                reads=["bc"], writes=[dk])

    p.dve(lambda e: e.memset(sb_hst[:], 0.0), reads=[],
          writes=["hst"])
    for ch in range(NCH):
        c0 = ch * TC
        p.dma(lambda e, c0=c0: e.dma_start(
            out=sb_dl[:], in_=dl_i[:, c0:c0 + TC].rearrange("(c p) m -> p c m", p=128)),
            reads=["dl_i"], writes=["dl"])
        p.dma(lambda e, c0=c0: e.dma_start(
            out=sb_u[:, :, 0:TC], in_=du_i[:, c0:c0 + TC]
                .rearrange("(c p) m -> p c m", p=128)),
            reads=["du_i"], writes=["u"])
        p.dma(lambda e, c0=c0: e.dma_start(out=sb_btc[:], in_=bt_i[:, c0:c0 + TC]),
              reads=["bt_i"], writes=["btc"])
        p.dma(lambda e, c0=c0: e.dma_start(out=sb_ctc[:], in_=ct_i[:, c0:c0 + TC]),
              reads=["ct_i"], writes=["ctc"])
        p.dve(lambda e: e.memset(sb_y[:], 0.0), reads=[],
              writes=[f"y{dh}" for dh in range(DIC)])
        for n in (range(DS) if "S" in _parts else []):
            p.mm(lambda e, n=n: e.matmul(
                p_bc2[:], h["sel"][:, n * 128:(n + 1) * 128], sb_btc[:],
                start=True, stop=True),
                reads=["sel", "btc", "const"], writes=["pbc2"])
            p.act(lambda e: e.activation(sb_brep[:], p_bc2[:], AF.Copy),
                  reads=["pbc2"], writes=["brep"])
            p.mm(lambda e, n=n: e.matmul(
                p_bc2[:], h["sel"][:, n * 128:(n + 1) * 128], sb_ctc[:],
                start=True, stop=True),
                reads=["sel", "ctc", "const"], writes=["pbc2"])
            p.act(lambda e: e.activation(sb_crep[:], p_bc2[:], AF.Copy),
                  reads=["pbc2"], writes=["crep"])
            for dh in range(DIC):
                col = dh * DS + n
                ib = dh % 2
                p.act(lambda e, dh=dh, col=col, ib=ib: e.activation(
                    sb_a[ib][:], sb_dl[:, dh, :], AF.Exp, scale=sb_A[:, col:col + 1]),
                    reads=["dl", "A"], writes=[f"a{ib}"])
                p.dve(lambda e, dh=dh, ib=ib: e.tensor_tensor(
                    sb_b[ib][:], sb_u[:, dh, 0:TC], sb_brep[:], ALU.mult),
                    reads=["u", "brep"], writes=[f"b{ib}"])
                p.dve(lambda e, col=col, ib=ib: e.tensor_tensor_scan(
                    sb_htr[ib][:], sb_a[ib][:], sb_b[ib][:], sb_hst[:, col:col + 1],
                    ALU.mult, ALU.add),
                    reads=[f"a{ib}", f"b{ib}", "hst"], writes=[f"htr{ib}"])
                p.dve(lambda e, col=col, ib=ib: e.tensor_copy(
                    sb_hst[:, col:col + 1], sb_htr[ib][:, TC - 1:TC]),
                    reads=[f"htr{ib}"], writes=["hst"])
                p.dve(lambda e, ib=ib: e.tensor_tensor(sb_yc[ib][:], sb_htr[ib][:],
                                                       sb_crep[:], ALU.mult),
                      reads=[f"htr{ib}", "crep"], writes=["ycx"])
                p.pool(lambda e, dh=dh, ib=ib: e.tensor_tensor(
                    sb_y[:, dh, :], sb_y[:, dh, :], sb_yc[ib][:], ALU.add),
                    reads=["ycx", f"y{dh}"], writes=[f"y{dh}"])
        p.dma(lambda e, c0=c0: e.dma_start(
            out=sb_u[:, :, 0:TC], in_=u_i[:, c0:c0 + TC]
                .rearrange("(c p) m -> p c m", p=128)),
            reads=["u_i"], writes=["u"])
        p.dma(lambda e, c0=c0: e.dma_start(
            out=sb_sr[:], in_=sr_i[:, c0:c0 + TC].rearrange("(c p) m -> p c m", p=128)),
            reads=["sr_i"], writes=["sr"])
        p.dma(lambda e, c0=c0: e.dma_start(
            out=sb_x[:, :, 0:TC], in_=x_ap[:, c0:c0 + TC]
                .rearrange("(c p) m -> p c m", p=128)),
            reads=[], writes=["x"])
        for dh in (range(DIC) if "O" in _parts else []):
            p.dve(lambda e, dh=dh: e.scalar_tensor_tensor(
                sb_y[:, dh, :], sb_u[:, dh, 0:TC], sb_dp[:, dh:dh + 1],
                sb_y[:, dh, :], ALU.mult, ALU.add),
                reads=["u", "dp", f"y{dh}"], writes=[f"y{dh}"])
            p.dve(lambda e, dh=dh: e.tensor_tensor(
                sb_sr[:, dh, :], sb_y[:, dh, :], sb_sr[:, dh, :], ALU.mult),
                reads=[f"y{dh}", "sr"], writes=["sr"])
        for cg in (range(D // TT) if "O" in _parts else []):
            stream_w(w["wout"], (0, 1024), (cg * TT, (cg + 1) * TT), sb_wd, 0, 0)
            stream_w(w["wout"], (1024, 2048), (cg * TT, (cg + 1) * TT), sb_wd, 8, 1)
            for fl in range(TT // 128):
                mc = cg * (TT // 128) + fl
                b = mc % 2
                for k in range(DIC):
                    p.mm(lambda e, k=k, fl=fl, b=b: e.matmul(
                        p_mm[b][0:128, 0:TC], sb_wd[:, k, fl * 128:(fl + 1) * 128],
                        sb_sr[:, k, :], start=(k == 0), stop=(k == DIC - 1)),
                        reads=["wd", "sr"], writes=[f"pmm{b}"])
                p.dve(lambda e, mc=mc, b=b: e.tensor_tensor(
                    sb_o[b][:, 0:TC], p_mm[b][0:128, 0:TC], sb_x[:, mc, 0:TC], ALU.add),
                    reads=[f"pmm{b}", "x"], writes=["o0"])
                p.dma2(lambda e, mc=mc, c0=c0, b=b: e.dma_start(
                    out=y_ap[mc * 128:(mc + 1) * 128, c0:c0 + TC], in_=sb_o[b][:, 0:TC]),
                    reads=["o0"], writes=["Y"])


def declare_l1_attn_weights(nc):
    DI, DS = 2048, 16
    w = {}
    w["win"] = nc.dram_tensor("ma_win", [D, 2 * DI], F32, kind="ExternalInput").ap()
    w["wdl"] = nc.dram_tensor("ma_wdl", [DI, DI], F32, kind="ExternalInput").ap()
    w["bdl"] = nc.dram_tensor("ma_bdl", [128, DI // 128], F32, kind="ExternalInput").ap()
    w["wB"] = nc.dram_tensor("ma_wB", [DI, DS], F32, kind="ExternalInput").ap()
    w["wC"] = nc.dram_tensor("ma_wC", [DI, DS], F32, kind="ExternalInput").ap()
    w["alog"] = nc.dram_tensor("ma_alog", [DI, DS], F32, kind="ExternalInput").ap()
    w["dpar"] = nc.dram_tensor("ma_dpar", [128, DI // 128], F32, kind="ExternalInput").ap()
    w["wout"] = nc.dram_tensor("ma_wout", [DI, D], F32, kind="ExternalInput").ap()
    return w


def l1_attn_weight_map(ins):
    return {
        "ma_win": f32(ins["L1_attn_W_in"]),
        "ma_wdl": f32(ins["L1_attn_W_delta"]),
        "ma_bdl": f32(np.asarray(ins["L1_attn_b_delta"]).reshape(16, 128).T),
        "ma_wB": f32(ins["L1_attn_W_B"]),
        "ma_wC": f32(ins["L1_attn_W_C"]),
        "ma_alog": f32(ins["L1_attn_A_log"]),
        "ma_dpar": f32(np.asarray(ins["L1_attn_D_param"]).reshape(16, 128).T),
        "ma_wout": f32(ins["L1_attn_W_out"]),
    }



def build_fused():
    nc = bass.Bass("TRN2", target_bir_lowering=False)
    x_d = nc.dram_tensor("x_fm", [D, T], F32, kind="ExternalInput").ap()
    y_d = nc.dram_tensor("y_fm", [D, T], F32, kind="ExternalOutput").ap()
    xs = [nc.dram_tensor(f"xs{i}", [D, T], F32, kind="Internal").ap()
          for i in range(1, 6)]
    w_l0f = declare_ffn_weights(nc, "l0", "l0f_")
    w_at = declare_l0_attn_weights(nc)
    w_l1f = declare_ffn_weights(nc, "l1", "l1f_")
    w_ma = declare_l1_attn_weights(nc)
    w1c_l0 = nc.dram_tensor("w1c_l0", [8, D, 4096], BF16, kind="Internal").ap()
    w2c_l0 = nc.dram_tensor("w2c_l0", [8, 4096, D], BF16, kind="Internal").ap()
    w1c_l1 = nc.dram_tensor("w1c_l1", [8, D, 4096], BF16, kind="Internal").ap()
    w2c_l1 = nc.dram_tensor("w2c_l1", [8, 2048, D], BF16, kind="Internal").ap()

    with contextlib.ExitStack() as st:
        h = alloc_common(nc, st)
        sems = {c: st.enter_context(nc.semaphore("sem_" + c)) for c in CHANNELS}
        block = st.enter_context(nc.Block())
        p = Prog(block, sems)
        declare_common(nc, p, h)
        alloc_ln(nc, st, h)
        phases = [
            lambda pst: emit_ffn(nc, p, h, pst, "l0", w_l0f, x_d, xs[0], "ph0_",
                                 (w1c_l0, w2c_l0), "fill"),
            lambda pst: emit_l0_attn(nc, p, h, pst, w_at, xs[0], xs[1]),
            lambda pst: emit_ffn(nc, p, h, pst, "l0", w_l0f, xs[1], xs[2], "ph2_",
                                 (w1c_l0, w2c_l0), "use"),
            lambda pst: emit_ffn(nc, p, h, pst, "l1", w_l1f, xs[2], xs[3], "ph3_",
                                 (w1c_l1, w2c_l1), "fill"),
            lambda pst: emit_l1_attn(nc, p, h, pst, w_ma, xs[3], xs[4]),
            lambda pst: emit_ffn(nc, p, h, pst, "l1", w_l1f, xs[4], y_d, "ph5_",
                                 (w1c_l1, w2c_l1), "use"),
        ]
        import os as _os
        _np = int(_os.environ.get("NPHASES", "6"))
        for ph in phases[:_np]:
            p.barrier(h)
            with contextlib.ExitStack() as pst:
                ph(pst)
                p.flush()
        p.finish()
    return nc


_prog_cache = {}


def _get_prog(name, builder):
    if name not in _prog_cache:
        _prog_cache[name] = builder()
    return _prog_cache[name]


def kernel(**inputs):
    x = np.asarray(inputs["L0_attn_X"], np.float32)
    shards = [f32(x[i].T) for i in range(N_CORES)]

    nc = _get_prog("fused", build_fused)
    wmap = dict(common_inputs())
    wmap.update(ffn_weight_map("l0", "l0f_", inputs))
    wmap.update(l0_attn_weight_map(inputs))
    wmap.update(ffn_weight_map("l1", "l1f_", inputs))
    wmap.update(l1_attn_weight_map(inputs))
    maps = []
    for s in shards:
        m = dict(wmap)
        m["x_fm"] = s
        maps.append(m)
    res = run_bass_kernel_spmd(nc, maps, core_ids=list(range(N_CORES)))
    out = np.stack([np.asarray(res.results[i]["y_fm"]).T for i in range(N_CORES)], 0)
    return np.ascontiguousarray(out, np.float32)
